# version 40
# baseline (speedup 1.0000x reference)
import numpy as np
from collections import deque
from contextlib import ExitStack
import concourse.bass as bass
import concourse.mybir as mybir
from concourse.bass_utils import run_bass_kernel_spmd

F32 = mybir.dt.float32
BF16 = mybir.dt.bfloat16
I32 = mybir.dt.int32
AF = mybir.ActivationFunctionType
ALU = mybir.AluOpType

B, S, D, M, H = 32, 2048, 1024, 128, 3072
NCORES = 8
BPC = B // NCORES
GPB = 2
NGRP = BPC // GPB
C = 128
DECAY, LR = 0.99, 0.01
NJ = H // 128
ND = D // 128
MAGIC = float(0x5F3759DF)
LAMC = float(DECAY ** C)
EPS4 = float(1e-5 * 127.0 ** 4)

ENGS = ("pe", "dve", "act", "sp", "pool")
NSLOT = 8


class Plan:
    def __init__(self):
        self.sem_names = list(ENGS) + [f"{q}{i}" for q in ("sp", "pool") for i in range(NSLOT)]
        self.ops = {e: [] for e in ENGS}
        self.cnt = {s: 0 for s in self.sem_names}
        self.seen = {e: {s: 0 for s in self.sem_names} for e in ENGS}
        self.opno = {e: 0 for e in ENGS}
        self.drained = {e: 0 for e in ENGS}
        self.wr = {}
        self.rd = {}
        self.pend = {e: ([], []) for e in ENGS}
        self.dma_idx = {"sp": 0, "pool": 0}

    def _wait(self, eng, sem, val, wopno=None):
        if sem == eng and sem in ENGS and wopno is not None:
            if eng == "pe":
                return
            if self.drained[eng] >= wopno:
                return
            self.ops[eng].append(("d",))
            self.drained[eng] = self.opno[eng]
            return
        if val <= self.seen[eng][sem]:
            return
        self.seen[eng][sem] = val
        self.ops[eng].append(("w", sem, val))

    def _haz(self, eng, reads, writes):
        for k in reads:
            w = self.wr.get(k)
            if w is not None:
                self._wait(eng, w[0], w[1], w[2] if len(w) > 2 else None)
        for k in writes:
            w = self.wr.get(k)
            if w is not None:
                self._wait(eng, w[0], w[1], w[2] if len(w) > 2 else None)
            for sem, vv in self.rd.get(k, {}).items():
                if isinstance(vv, tuple):
                    self._wait(eng, sem, vv[0], vv[1])
                else:
                    self._wait(eng, sem, vv)

    def emit(self, eng, build, reads=(), writes=(), inc=1):
        if eng in ("sp", "pool"):
            self._haz(eng, reads, writes)
            i = self.dma_idx[eng]
            self.dma_idx[eng] = i + 1
            slot = f"{eng}{i % NSLOT}"
            self._wait(eng, slot, 16 * (i // NSLOT))
            self.cnt[slot] += 16
            val = self.cnt[slot]
            self.ops[eng].append(("o", build, ("dma", slot)))
            for k in reads:
                self.rd.setdefault(k, {})[slot] = val
            for k in writes:
                self.wr[k] = (slot, val)
                self.rd[k] = {}
            return
        self._haz(eng, reads, writes)
        self.opno[eng] += 1
        if inc == 0:
            pr, pw = self.pend[eng]
            pr.extend(reads)
            pw.extend(writes)
            self.ops[eng].append(("o", build, None))
            return
        self.cnt[eng] += inc
        n = self.cnt[eng]
        o = self.opno[eng]
        self.ops[eng].append(("o", build, ("eng", eng)))
        pr, pw = self.pend[eng]
        for k in list(reads) + pr:
            self.rd.setdefault(k, {})[eng] = (n, o)
        for k in list(writes) + pw:
            self.wr[k] = (eng, n, o)
            self.rd[k] = {}
        self.pend[eng] = ([], [])


def _build(nch=S // C):
    Sx = nch * C
    nc = bass.Bass("TRN2", target_bir_lowering=False)

    xd = nc.dram_tensor("x", [BPC, Sx, D], F32, kind="ExternalInput")
    wqT = nc.dram_tensor("wqT", [D, M], F32, kind="ExternalInput")
    wkT = nc.dram_tensor("wkT", [D, M], F32, kind="ExternalInput")
    wvT = nc.dram_tensor("wvT", [D, M], F32, kind="ExternalInput")
    woT = nc.dram_tensor("woT", [M, D], F32, kind="ExternalInput")
    bkc = nc.dram_tensor("bkc", [M, 1], F32, kind="ExternalInput")
    bqc = nc.dram_tensor("bqc", [M, 1], F32, kind="ExternalInput")
    bvc = nc.dram_tensor("bvc", [M, 1], F32, kind="ExternalInput")
    bor = nc.dram_tensor("bor", [1, D], F32, kind="ExternalInput")
    onesr = nc.dram_tensor("onesr", [1, 128], F32, kind="ExternalInput")
    wgb = nc.dram_tensor("wgb", [NJ, ND, 128, 128], I32, kind="ExternalInput")
    wub = nc.dram_tensor("wub", [NJ, ND, 128, 128], I32, kind="ExternalInput")
    wdb = nc.dram_tensor("wdb", [NJ, 128, D], I32, kind="ExternalInput")
    identd = nc.dram_tensor("identd", [128, 128], F32, kind="ExternalInput")
    lamqd = nc.dram_tensor("lamqd", [128, GPB * C], F32, kind="ExternalInput")
    wmtd = nc.dram_tensor("wmtd", [128, 128], F32, kind="ExternalInput")
    lamkd = nc.dram_tensor("lamkd", [128, 1], F32, kind="ExternalInput")
    outd = nc.dram_tensor("out", [BPC, Sx, D], F32, kind="ExternalOutput")

    es = ExitStack()
    sbuf = lambda nm, sh, dt: es.enter_context(nc.sbuf_tensor(nm, sh, dt))
    psum = lambda nm, sh, dt: es.enter_context(nc.psum_tensor(nm, sh, dt))

    with es:
        wg_sb = sbuf("wg_sb", [128, NJ * 1024], BF16)
        wu_sb = sbuf("wu_sb", [128, NJ * 1024], BF16)
        wd_sb = sbuf("wd_sb", [128, NJ * 1024], BF16)
        wq_sb = sbuf("wq_sb", [128, ND * 128], BF16)
        wk_sb = sbuf("wk_sb", [128, ND * 128], BF16)
        wv_sb = sbuf("wv_sb", [128, ND * 128], BF16)
        wo_sb = sbuf("wo_sb", [128, D], BF16)
        X = [sbuf("X0", [128, GPB * D], F32), sbuf("X1", [128, GPB * D], F32),
             sbuf("X2", [128, GPB * D], F32)]
        xhat = [sbuf("xhat0", [128, D], BF16), sbuf("xhat1", [128, D], BF16)]
        x1T = sbuf("x1T", [128, ND * GPB * C], BF16)
        x3T = sbuf("x3T", [128, ND * GPB * C], BF16)
        qt = sbuf("qt", [128, GPB * C], BF16)
        kt = sbuf("kt", [128, GPB * C], BF16)
        vt = sbuf("vt", [128, GPB * C], BF16)
        khat = sbuf("khat", [128, GPB * C], BF16)
        vtm = sbuf("vtm", [128, GPB * C], BF16)
        pwt = sbuf("pwt", [128, GPB * C], BF16)
        ot = sbuf("ot", [128, GPB * C], BF16)
        stf = sbuf("stf", [128, BPC * C], F32)
        stb = sbuf("stb", [128, BPC * C], BF16)
        ytA = sbuf("ytA", [128, D], BF16)
        ytB = sbuf("ytB", [128, D], BF16)
        hT = sbuf("hT", [128, 2 * GPB * C], BF16)
        rowsm = sbuf("rowsm", [128, D], BF16)
        ones_sb = sbuf("ones_sb", [1, 128], BF16)
        identb = sbuf("identb", [128, 128], BF16)
        lamq = sbuf("lamq", [128, GPB * C], BF16)
        wmt = sbuf("wmt", [128, 128], BF16)
        lamk = sbuf("lamk", [128, 1], F32)
        bk_s = sbuf("bk_s", [M, 1], F32)
        bq_s = sbuf("bq_s", [M, 1], F32)
        bv_s = sbuf("bv_s", [M, 1], F32)
        _bn6A = sbuf("bn6_A", [128, 12 * GPB], F32)
        _bn6B = sbuf("bn6_B", [128, 12 * GPB], F32)
        bn6 = [_bn6A, _bn6B, _bn6A, _bn6B]
        agg = [sbuf(f"agg_{i}", [128, 2 * GPB], F32) for i in range(4)]
        uu = [sbuf(f"uu_{i}", [128, GPB], F32) for i in range(4)]
        tt = [sbuf(f"tt_{i}", [128, GPB], F32) for i in range(4)]
        ui = [sbuf(f"ui_{i}", [128, GPB], I32) for i in range(4)]
        rr = [sbuf(f"rr_{i}", [128, GPB], F32) for i in range(4)]
        mrc = [sbuf(f"mrc_{i}", [128, GPB], F32) for i in range(4)]
        stx = [sbuf(f"stx_{i}", [128, 4 * GPB], F32) for i in (0, 2)]
        stxm = {0: 0, 2: 1}

        bank = [psum(f"pb{i}", [128, 512], F32) for i in range(8)]
        bankbf = [b[:].bitcast(BF16) for b in bank]

        P = Plan()
        sems = {}
        for s in P.sem_names:
            sems[s] = es.enter_context(nc.semaphore(f"sem_{s}"))

        def mm(build, reads, wbank, inc):
            P.emit("pe", build, reads=reads, writes=(("bank", wbank),), inc=inc)

        P.emit("pool", lambda E: E.dma_start(wq_sb[:], wqT[:, :].rearrange("(dc p) m -> p dc m", p=128)), writes=(("wq",),))
        P.emit("pool", lambda E: E.dma_start(wk_sb[:], wkT[:, :].rearrange("(dc p) m -> p dc m", p=128)), writes=(("wk",),))
        P.emit("pool", lambda E: E.dma_start(wv_sb[:], wvT[:, :].rearrange("(dc p) m -> p dc m", p=128)), writes=(("wv",),))
        P.emit("pool", lambda E: E.dma_start(wo_sb[:], woT[:, :]), writes=(("wo",),))
        P.emit("pool", lambda E: E.dma_start(identb[:], identd[:, :]), writes=(("ident",),))
        P.emit("pool", lambda E: E.dma_start(lamq[:], lamqd[:, :]), writes=(("lamq",),))
        P.emit("pool", lambda E: E.dma_start(wmt[:], wmtd[:, :]), writes=(("wmt",),))
        P.emit("pool", lambda E: E.dma_start(rowsm[0:1, :], bor[:, :]), writes=(("bo",),))
        P.emit("pool", lambda E: E.dma_start(ones_sb[:, :], onesr[:, :]), writes=(("ones",),))
        P.emit("sp", lambda E: E.dma_start(lamk[:], lamkd[:, :]), writes=(("lamk",),))
        P.emit("sp", lambda E: E.dma_start(bk_s[:], bkc[:, :]), writes=(("bk",),))
        P.emit("sp", lambda E: E.dma_start(bq_s[:], bqc[:, :]), writes=(("bq",),))
        P.emit("sp", lambda E: E.dma_start(bv_s[:], bvc[:, :]), writes=(("bv",),))
        P.emit("dve", lambda E: E.memset(stf[:], 0.0), writes=tuple(("stf", b) for b in range(BPC)))
        P.emit("dve", lambda E: E.memset(stb[:], 0.0), writes=tuple(("stb", b) for b in range(BPC)))

        for j in range(NJ):
            src_g = wgb[j].transpose([1, 0, 2])
            src_u = wub[j].transpose([1, 0, 2])
            P.emit("pool", lambda E, s=src_g, j=j: E.dma_start(
                wg_sb[:, j * 1024:(j + 1) * 1024].rearrange("p (dc h) -> p dc h", dc=ND), s), writes=(("wg", j),))
            P.emit("pool", lambda E, s=src_u, j=j: E.dma_start(
                wu_sb[:, j * 1024:(j + 1) * 1024].rearrange("p (dc h) -> p dc h", dc=ND), s), writes=(("wu", j),))
            P.emit("pool", lambda E, j=j: E.dma_start(wd_sb[:, j * 1024:(j + 1) * 1024], wdb[j]), writes=(("wd", j),))

        def rsqrt_step_list(pt, cols, eps):
            u, t_, i_, r_ = uu[pt], tt[pt], ui[pt], rr[pt]
            lo, hi = cols.start, cols.stop
            ku, kt_, ki = ("uu", pt, lo), ("tt", pt, lo), ("ui", pt, lo)
            vsrc = agg[pt][:, 2 * lo + 1: 2 * hi: 2]
            vkeys = tuple(("agg", pt, b) for b in range(lo, hi))
            cs = slice(lo, hi)
            y = i_[:, cs].bitcast(F32)
            steps = [
                lambda: P.emit("dve", lambda E: E.tensor_scalar(u[:, cs], vsrc, eps, None, ALU.add), reads=vkeys, writes=(ku,)),
                lambda: P.emit("dve", lambda E: E.tensor_copy(t_[:, cs], u[:, cs].bitcast(I32)), reads=(ku,), writes=(kt_,)),
                lambda: P.emit("dve", lambda E: E.tensor_scalar(t_[:, cs], t_[:, cs], -0.5, MAGIC, ALU.mult, ALU.add), reads=(kt_,), writes=(kt_,)),
                lambda: P.emit("dve", lambda E: E.tensor_copy(i_[:, cs], t_[:, cs]), reads=(kt_,), writes=(ki,)),
            ]
            for _ in range(3):
                steps += [
                    lambda: P.emit("dve", lambda E: E.tensor_mul(t_[:, cs], y, y), reads=(ki,), writes=(kt_,)),
                    lambda: P.emit("dve", lambda E: E.tensor_mul(t_[:, cs], t_[:, cs], u[:, cs]), reads=(kt_, ku), writes=(kt_,)),
                    lambda: P.emit("dve", lambda E: E.tensor_scalar(t_[:, cs], t_[:, cs], -0.5, 1.5, ALU.mult, ALU.add), reads=(kt_,), writes=(kt_,)),
                    lambda: P.emit("dve", lambda E: E.tensor_mul(y, y, t_[:, cs]), reads=(ki, kt_), writes=(ki,)),
                ]
            steps.append(lambda: P.emit("dve", lambda E: E.tensor_copy(r_[:, cs], y), reads=(ki,),
                                        writes=tuple(("rr", pt, b) for b in range(lo, hi))))
            return steps

        def bn_from(pt, lb, src_halves, src_keys):
            steps = []
            for hh in (0, 1):
                steps.append(lambda hh=hh: P.emit("dve", lambda E: E.bn_stats(
                    bn6[pt][:, lb * 12 + hh * 6: lb * 12 + hh * 6 + 6], src_halves[hh]),
                    reads=src_keys[hh], writes=(("bn6", pt, lb, hh),)))
            steps.append(lambda: P.emit("dve", lambda E: E.bn_aggr(
                agg[pt][:, 2 * lb:2 * lb + 2], bn6[pt][:, lb * 12: lb * 12 + 12]),
                reads=(("bn6", pt, lb, 0), ("bn6", pt, lb, 1)), writes=(("agg", pt, lb),)))
            return steps

        def stats_act(pt, lb, src_halves, src_keys, trash_ap, trash_key):
            sx = stx[stxm[pt]]
            steps = []
            for hh in (0, 1):
                steps.append(lambda hh=hh: P.emit("act", lambda E: E.activation(
                    trash_ap, src_halves[hh], AF.Identity,
                    accum_out=sx[:, lb * 4 + hh: lb * 4 + hh + 1]),
                    reads=src_keys[hh], writes=(trash_key, ("x1T", 0, 1), ("sx", pt, lb, hh))))
                steps.append(lambda hh=hh: P.emit("act", lambda E: E.activation(
                    trash_ap, src_halves[hh], AF.Square,
                    accum_out=sx[:, lb * 4 + 2 + hh: lb * 4 + 3 + hh]),
                    reads=src_keys[hh], writes=(trash_key, ("x1T", 0, 1), ("sx", pt, lb, 2 + hh))))
            tcol = tt[pt][:, lb:lb + 1]
            mcol = agg[pt][:, 2 * lb:2 * lb + 1]
            vcol = agg[pt][:, 2 * lb + 1:2 * lb + 2]
            mmcol = mrc[pt][:, lb:lb + 1]
            steps.append(lambda: P.emit("dve", lambda E: E.tensor_add(
                tcol, sx[:, lb * 4:lb * 4 + 1], sx[:, lb * 4 + 1:lb * 4 + 2]),
                reads=(("sx", pt, lb, 0), ("sx", pt, lb, 1)), writes=(("tt", pt, lb),)))
            steps.append(lambda: P.emit("dve", lambda E: E.tensor_scalar(
                mcol, tcol, 1.0 / D, None, ALU.mult),
                reads=(("tt", pt, lb),), writes=(("aggm", pt, lb),)))
            steps.append(lambda: P.emit("dve", lambda E: E.tensor_add(
                tcol, sx[:, lb * 4 + 2:lb * 4 + 3], sx[:, lb * 4 + 3:lb * 4 + 4]),
                reads=(("sx", pt, lb, 2), ("sx", pt, lb, 3)), writes=(("tt", pt, lb),)))
            steps.append(lambda: P.emit("dve", lambda E: E.tensor_mul(mmcol, mcol, mcol),
                reads=(("aggm", pt, lb),), writes=(("mrc", pt, lb),)))
            steps.append(lambda: P.emit("dve", lambda E: E.scalar_tensor_tensor(
                vcol, tcol, 1.0 / D, mmcol, ALU.mult, ALU.subtract),
                reads=(("tt", pt, lb), ("mrc", pt, lb)), writes=(("agg", pt, lb),)))
            return steps

        def mr_step(pt, lb):
            return lambda: P.emit("dve", lambda E: E.tensor_mul(
                mrc[pt][:, lb:lb + 1], agg[pt][:, 2 * lb:2 * lb + 1], rr[pt][:, lb:lb + 1]),
                reads=(("agg", pt, lb), ("aggm", pt, lb), ("rr", pt, lb)), writes=(("mrc", pt, lb),))

        def build_A(ti, g, cidx):
            par = ti % 3
            Xc = X[par]
            A = []

            def xk(lb):
                return ("X", par, lb)

            Adma = []
            A.append(("EPI_BARRIER", ti - 3))
            for lb in range(GPB):
                gb = 2 * g + lb
                A.append(lambda lb=lb, gb=gb: P.emit(
                    "sp", lambda E: E.dma_start(Xc[:, lb * D:(lb + 1) * D], xd[gb, cidx * C:(cidx + 1) * C, :]),
                    writes=(xk(lb),)))
            for lb in range(GPB):
                A += stats_act(0, lb, [Xc[:, lb * D:lb * D + 512], Xc[:, lb * D + 512:lb * D + 1024]],
                               [(xk(lb),), (xk(lb),)], x1T[:, 0:512], ("x1T", 0, 0))
            A += rsqrt_step_list(0, slice(0, GPB), 1e-5)
            A.append(("PRO_BARRIER", ti - 1))
            for lb in range(GPB):
                A.append(lambda lb=lb: P.emit("dve", lambda E: E.tensor_scalar(
                    xhat[lb][:], Xc[:, lb * D:(lb + 1) * D],
                    agg[0][:, 2 * lb:2 * lb + 1], rr[0][:, lb:lb + 1], ALU.subtract, ALU.mult),
                    reads=(xk(lb), ("agg", 0, lb), ("aggm", 0, lb), ("rr", 0, lb)), writes=(("xhat", lb),)))
            for lb in range(GPB):
                bk_ = 4 + lb

                def trs(lb=lb, bk_=bk_):
                    for dc in range(ND):
                        P.emit("pe", lambda E, dc=dc, lb=lb, bk_=bk_: E.transpose(
                            bankbf[bk_][:, dc * 128:(dc + 1) * 128], xhat[lb][:, dc * 128:(dc + 1) * 128], identb[:]),
                            reads=(("xhat", lb), ("ident",)), writes=(("bank", bk_),), inc=1 if dc == ND - 1 else 0)
                A.append(trs)
                A.append(lambda lb=lb, bk_=bk_: P.emit("dve", lambda E: E.tensor_copy(
                    x1T[:].rearrange("p (dc bt) -> p dc bt", dc=ND)[:, :, lb * 128:(lb + 1) * 128],
                    bankbf[bk_].rearrange("p (dc t) -> p dc t", dc=ND)),
                    reads=(("bank", bk_),), writes=tuple(("x1T", dc, lb) for dc in range(ND))))
            x1keys = tuple(("x1T", dc, lb) for dc in range(ND) for lb in range(GPB))

            def proj(wsb, wkey, bnk):
                def f():
                    for dc in range(ND):
                        mm(lambda E, dc=dc: E.matmul(
                            bank[bnk][:, 0:GPB * C], wsb[:, dc * 128:(dc + 1) * 128],
                            x1T[:, dc * GPB * C:(dc + 1) * GPB * C],
                            start=dc == 0, stop=dc == ND - 1, skip_group_check=True),
                            reads=(wkey,) + x1keys, wbank=bnk, inc=1 if dc == ND - 1 else 0)
                return f

            A.append(proj(wq_sb, ("wq",), 4))
            A.append(lambda: P.emit("act", lambda E: E.activation(
                qt[:], bank[4][:, 0:GPB * C], AF.Identity, bias=bq_s[:]),
                reads=(("bank", 4), ("bq",)), writes=(("qt",),)))
            A.append(lambda: P.emit("dve", lambda E: E.tensor_tensor(
                qt[:], qt[:], lamq[:], ALU.mult),
                reads=(("qt",), ("lamq",)), writes=(("qt",),)))
            A.append(proj(wk_sb, ("wk",), 5))
            A.append(lambda: P.emit("act", lambda E: E.activation(
                kt[:], bank[5][:, 0:GPB * C], AF.Identity, bias=bk_s[:]),
                reads=(("bank", 5), ("bk",)), writes=(("kt",),)))
            A.append(proj(wv_sb, ("wv",), 4))
            A.append(lambda: P.emit("act", lambda E: E.activation(
                vt[:], bank[4][:, 0:GPB * C], AF.Identity, bias=bv_s[:]),
                reads=(("bank", 4), ("bv",)), writes=(("vt",),)))
            for lb in range(GPB):
                A.append(lambda lb=lb: P.emit("pe", lambda E: E.transpose(
                    bankbf[5][:, lb * 128:(lb + 1) * 128], kt[:, lb * 128:(lb + 1) * 128], identb[:]),
                    reads=(("kt",), ("ident",)), writes=(("bank", 5),)))
                A.append(lambda lb=lb: P.emit("pe", lambda E: E.transpose(
                    bankbf[4][:, lb * 128:(lb + 1) * 128], vt[:, lb * 128:(lb + 1) * 128], identb[:]),
                    reads=(("vt",), ("ident",)), writes=(("bank", 4),)))
            for lb in range(GPB):
                A.append(lambda lb=lb: P.emit("dve", lambda E: E.tensor_scalar(
                    khat[:, lb * 128:(lb + 1) * 128], bankbf[5][:, lb * 128:(lb + 1) * 128], lamk[:], None, ALU.mult),
                    reads=(("bank", 5), ("lamk",)), writes=(("khat", lb),)))
                A.append(lambda lb=lb: P.emit("act", lambda E: E.activation(
                    vtm[:, lb * 128:(lb + 1) * 128], bankbf[4][:, lb * 128:(lb + 1) * 128], AF.Identity),
                    reads=(("bank", 4),), writes=(("vtm", lb),)))
            for lb in range(GPB):
                gb = 2 * g + lb
                bs = slice(lb * 128, (lb + 1) * 128)
                A.append(lambda bs=bs: P.emit("pe", lambda E: E.matmul(
                    bank[5][:, 0:128], kt[:, bs], qt[:, bs], start=True, stop=True, skip_group_check=True),
                    reads=(("kt",), ("qt",)), writes=(("bank", 5),)))
                A.append(lambda lb=lb, bs=bs: P.emit("dve", lambda E: E.tensor_tensor(
                    pwt[:, bs], bank[5][:, 0:128], wmt[:], ALU.mult),
                    reads=(("bank", 5), ("wmt",)), writes=(("pwt", lb),)))

                def omm(lb=lb, gb=gb, bs=bs):
                    gs = slice(gb * 128, (gb + 1) * 128)
                    mm(lambda E: E.matmul(bank[4][:, 0:128], vtm[:, bs], pwt[:, bs],
                                          start=True, stop=False, skip_group_check=True),
                       reads=(("vtm", lb), ("pwt", lb)), wbank=4, inc=0)
                    mm(lambda E: E.matmul(bank[4][:, 0:128], stb[:, gs], qt[:, bs],
                                          start=False, stop=True, skip_group_check=True),
                       reads=(("stb", gb), ("qt",)), wbank=4, inc=1)
                A.append(omm)
                A.append(lambda lb=lb, bs=bs: P.emit("act", lambda E: E.activation(
                    ot[:, bs], bank[4][:, 0:128], AF.Identity), reads=(("bank", 4),), writes=(("ot", lb),)))
                A.append(lambda lb=lb, bs=bs: P.emit("pe", lambda E: E.matmul(
                    bank[5][:, 128:256], khat[:, bs], vtm[:, bs], start=True, stop=True, skip_group_check=True),
                    reads=(("khat", lb), ("vtm", lb)), writes=(("bank", 5),)))

                def stup(gb=gb):
                    gs = slice(gb * 128, (gb + 1) * 128)
                    P.emit("dve", lambda E: E.tensor_scalar(stf[:, gs], stf[:, gs], LAMC, None, ALU.mult),
                           reads=(("stf", gb),), writes=(("stf", gb),))
                    P.emit("dve", lambda E: E.tensor_tensor(stf[:, gs], stf[:, gs], bank[5][:, 128:256], ALU.add),
                           reads=(("stf", gb), ("bank", 5)), writes=(("stf", gb),))
                    P.emit("dve", lambda E: E.tensor_copy(stb[:, gs], stf[:, gs]),
                           reads=(("stf", gb),), writes=(("stb", gb),))
                A.append(stup)
            for lb in range(GPB):
                bs = slice(lb * 128, (lb + 1) * 128)

                def y1mm(lb=lb, bs=bs):
                    for dh in (0, 1):
                        mm(lambda E, dh=dh: E.matmul(bank[4 + dh][:], ot[:, bs],
                                                     wo_sb[:, dh * 512:(dh + 1) * 512],
                                                     start=True, stop=False, skip_group_check=True),
                           reads=(("ot", lb), ("wo",)), wbank=4 + dh, inc=0)
                        mm(lambda E, dh=dh: E.matmul(bank[4 + dh][:], ones_sb[0:1, :],
                                                     rowsm[0:1, dh * 512:(dh + 1) * 512],
                                                     start=False, stop=True, skip_group_check=True),
                           reads=(("bo",), ("ones",)), wbank=4 + dh, inc=1)
                A.append(y1mm)
                A += bn_from(1, lb, [bank[4][:], bank[5][:]], [(("bank", 4),), (("bank", 5),)])
                A += rsqrt_step_list(1, slice(lb, lb + 1), 1e-5)
                A.append(mr_step(1, lb))

                def apply2(lb=lb):
                    P.emit("dve", lambda E: E.tensor_scalar(
                        Xc[:, lb * D:(lb + 1) * D], Xc[:, lb * D:(lb + 1) * D],
                        mrc[1][:, lb:lb + 1], None, ALU.subtract),
                        reads=(xk(lb), ("mrc", 1, lb)), writes=(xk(lb),))
                    for dh in (0, 1):
                        P.emit("dve", lambda E, dh=dh: E.scalar_tensor_tensor(
                            Xc[:, lb * D + dh * 512: lb * D + (dh + 1) * 512],
                            bank[4 + dh][:], rr[1][:, lb:lb + 1],
                            Xc[:, lb * D + dh * 512: lb * D + (dh + 1) * 512],
                            ALU.mult, ALU.add),
                            reads=(("bank", 4 + dh), ("rr", 1, lb), (xk(lb),)), writes=(xk(lb),))
                A.append(apply2)
                A += stats_act(2, lb, [Xc[:, lb * D:lb * D + 512], Xc[:, lb * D + 512:lb * D + 1024]],
                               [(xk(lb),), (xk(lb),)], x1T[:, 0:512], ("x1T", 0, 0))
            A += rsqrt_step_list(2, slice(0, GPB), 1e-5)
            for lb in range(GPB):
                A.append(lambda lb=lb: P.emit("dve", lambda E: E.tensor_scalar(
                    xhat[lb][:], Xc[:, lb * D:(lb + 1) * D],
                    agg[2][:, 2 * lb:2 * lb + 1], rr[2][:, lb:lb + 1], ALU.subtract, ALU.mult),
                    reads=(xk(lb), ("agg", 2, lb), ("aggm", 2, lb), ("rr", 2, lb)),
                    writes=(("xhat", lb),)))
            A.append(("TILE_DONE", ti))
            return A

        EPI_BANKS = deque()
        EPI_REST = deque()
        AQ = deque()

        a_done = set()
        epi_done = set()
        pro_done = set()

        def drain(q, k):
            while q and k > 0:
                h = q[0]
                if isinstance(h, tuple) and h[0] == "EPI_BARRIER":
                    if h[1] >= 0 and h[1] not in epi_done:
                        return
                    q.popleft()
                    continue
                if isinstance(h, tuple) and h[0] == "TILE_DONE":
                    a_done.add(h[1])
                    q.popleft()
                    continue
                if isinstance(h, tuple) and h[0] == "EPI_DONE":
                    epi_done.add(h[1])
                    q.popleft()
                    continue
                if isinstance(h, tuple) and h[0] == "PRO_BARRIER":
                    if h[1] >= 0 and h[1] not in pro_done:
                        return
                    q.popleft()
                    continue
                q.popleft()()
                k -= 1

        def drain_until_tile(ti):
            while ti not in a_done:
                h = AQ[0]
                if isinstance(h, tuple) and h[0] == "PRO_BARRIER":
                    assert h[1] < 0 or h[1] in pro_done, (h,)
                    AQ.popleft()
                    continue
                if isinstance(h, tuple) and h[0] == "EPI_BARRIER":
                    if h[1] >= 0 and h[1] not in epi_done:
                        drain(EPI_BANKS, 10 ** 9)
                        drain(EPI_REST, 10 ** 9)
                        assert h[1] < 0 or h[1] in epi_done, (h,)
                    AQ.popleft()
                    continue
                if isinstance(h, tuple) and h[0] == "TILE_DONE":
                    a_done.add(h[1])
                    AQ.popleft()
                    continue
                AQ.popleft()()

        def emit_F(ti, g, cidx):
            par = ti % 3
            Xc = X[par]

            def xk(lb):
                return ("X", par, lb)

            drain_until_tile(ti)
            drain(EPI_BANKS, 99)
            for lb in range(GPB):
                bk_ = 2 + lb
                for dc in range(ND):
                    P.emit("pe", lambda E, lb=lb, dc=dc, bk_=bk_: E.transpose(
                        bankbf[bk_][:, dc * 128:(dc + 1) * 128], xhat[lb][:, dc * 128:(dc + 1) * 128], identb[:]),
                        reads=(("xhat", lb), ("ident",)), writes=(("bank", bk_),), inc=1 if dc == ND - 1 else 0)
                P.emit("dve", lambda E, lb=lb, bk_=bk_: E.tensor_copy(
                    x3T[:].rearrange("p (dc bt) -> p dc bt", dc=ND)[:, :, lb * 128:(lb + 1) * 128],
                    bankbf[bk_].rearrange("p (dc t) -> p dc t", dc=ND)),
                    reads=(("bank", bk_),), writes=tuple(("x3T", dc, lb) for dc in range(ND)))
            x3keys = tuple(("x3T", dc, lb) for dc in range(ND) for lb in range(GPB))
            pro_done.add(ti)

            def emit_down(j):
                jp = j % 2
                for lb in range(GPB):
                    for dh in range(2):
                        dbk = (6, 7, 0, 1)[lb * 2 + dh]
                        mm(lambda E, j=j, jp=jp, lb=lb, dh=dh, dbk=dbk: E.matmul(
                            bank[dbk][:], hT[:, jp * 256 + lb * 128: jp * 256 + (lb + 1) * 128],
                            wd_sb[:, j * 1024 + dh * 512: j * 1024 + (dh + 1) * 512],
                            start=j == 0, stop=j == NJ - 1, skip_group_check=True),
                            reads=(("hT", jp), ("wd", j)), wbank=dbk,
                            inc=1 if j == NJ - 1 else 0)

            for j in range(NJ):
                jp = j % 2
                for dc in range(ND):
                    mm(lambda E, j=j, dc=dc: E.matmul(
                        bank[2][:, 0:256], wg_sb[:, j * 1024 + dc * 128: j * 1024 + (dc + 1) * 128],
                        x3T[:, dc * 256:(dc + 1) * 256], start=dc == 0, stop=dc == ND - 1,
                        skip_group_check=True),
                        reads=(("wg", j),) + x3keys, wbank=2, inc=1 if dc == ND - 1 else 0)
                P.emit("act", lambda E, jp=jp: E.activation(
                    hT[:, jp * 256:(jp + 1) * 256], bank[2][:, 0:256], AF.Silu, scale=1.0 / 127.0),
                    reads=(("bank", 2),), writes=(("hT", jp),))
                if j >= 1:
                    drain(AQ, 3)
                for dc in range(ND):
                    mm(lambda E, j=j, dc=dc: E.matmul(
                        bank[3][:, 0:256], wu_sb[:, j * 1024 + dc * 128: j * 1024 + (dc + 1) * 128],
                        x3T[:, dc * 256:(dc + 1) * 256], start=dc == 0, stop=dc == ND - 1,
                        skip_group_check=True),
                        reads=(("wu", j),) + x3keys, wbank=3, inc=1 if dc == ND - 1 else 0)
                if j > 0:
                    emit_down(j - 1)
                if j >= 1:
                    drain(AQ, 2)
                P.emit("dve", lambda E, jp=jp: E.tensor_tensor(
                    hT[:, jp * 256:(jp + 1) * 256], hT[:, jp * 256:(jp + 1) * 256], bank[3][:, 0:256], ALU.mult),
                    reads=(("hT", jp), ("bank", 3)), writes=(("hT", jp),))
                drain(EPI_REST, 4)
                if j >= 1:
                    drain(AQ, 3)
            emit_down(NJ - 1)

            yts = (ytA, ytB)
            for lb in range(GPB):
                for dh in range(2):
                    dbk = (6, 7, 0, 1)[lb * 2 + dh]
                    EPI_BANKS.append(lambda lb=lb, dh=dh, dbk=dbk: P.emit(
                        "act", lambda E: E.activation(yts[lb][:, dh * 512:(dh + 1) * 512], bank[dbk][:], AF.Identity),
                        reads=(("bank", dbk),), writes=(("yt", lb, dh),)))
            for lb in range(GPB):
                for st in bn_from(3, lb, [yts[lb][:, 0:512], yts[lb][:, 512:1024]],
                                  [(("yt", lb, 0),), (("yt", lb, 1),)]):
                    EPI_REST.append(st)
            for st in rsqrt_step_list(3, slice(0, GPB), EPS4):
                EPI_REST.append(st)
            for lb in range(GPB):
                EPI_REST.append(mr_step(3, lb))
            for lb in range(GPB):
                gb = 2 * g + lb

                def apply4(lb=lb):
                    for dh in (0, 1):
                        P.emit("dve", lambda E, dh=dh: E.tensor_scalar(
                            yts[lb][:, dh * 512:(dh + 1) * 512], yts[lb][:, dh * 512:(dh + 1) * 512],
                            rr[3][:, lb:lb + 1], mrc[3][:, lb:lb + 1], ALU.mult, ALU.subtract),
                            reads=(("yt", lb, dh), ("rr", 3, lb), ("mrc", 3, lb)), writes=(("yt", lb, dh),))
                        P.emit("dve", lambda E, dh=dh: E.tensor_tensor(
                            Xc[:, lb * D + dh * 512: lb * D + (dh + 1) * 512],
                            Xc[:, lb * D + dh * 512: lb * D + (dh + 1) * 512],
                            yts[lb][:, dh * 512:(dh + 1) * 512], ALU.add),
                            reads=(("yt", lb, dh), (xk(lb),)), writes=(xk(lb),))
                EPI_REST.append(apply4)
                EPI_REST.append(lambda lb=lb, gb=gb: P.emit(
                    "sp", lambda E: E.dma_start(outd[gb, cidx * C:(cidx + 1) * C, :], Xc[:, lb * D:(lb + 1) * D]),
                    reads=(xk(lb),)))
            EPI_REST.append(("EPI_DONE", ti))

        tiles = [(g, c) for c in range(nch) for g in range(NGRP)]
        AQ.extend(build_A(0, *tiles[0]))
        for ti, (g, c) in enumerate(tiles):
            if ti + 1 < len(tiles):
                AQ.extend(build_A(ti + 1, *tiles[ti + 1]))
            emit_F(ti, g, c)
        if tiles:
            drain_until_tile(len(tiles) - 1)
        while EPI_BANKS or EPI_REST or AQ:
            drain(EPI_BANKS, 10 ** 9)
            drain(EPI_REST, 10 ** 9)
            drain(AQ, 10 ** 9)

        for i in range(NSLOT):
            for q in ("sp", "pool"):
                s_ = f"{q}{i}"
                if P.cnt[s_]:
                    P.ops["sp"].append(("w", s_, P.cnt[s_]))

        waited = {s: set() for s in P.sem_names}
        for e in ENGS:
            for item in P.ops[e]:
                if item[0] == "w":
                    waited[item[1]].add(item[2])
        for e in ENGS:
            kept = []
            n = 0
            newops = []
            for item in P.ops[e]:
                if item[0] == "o" and item[2] is not None and item[2][0] == "eng":
                    n += 1
                    if n in waited[item[2][1]]:
                        kept.append(n)
                        newops.append(item)
                    else:
                        newops.append(("o", item[1], None))
                else:
                    newops.append(item)
            P.ops[e] = newops
            P.kept_counts = getattr(P, "kept_counts", {})
            P.kept_counts[e] = kept
        import bisect
        for e in ENGS:
            out = []
            for item in P.ops[e]:
                if item[0] == "w" and item[1] in ENGS:
                    kc = P.kept_counts[item[1]]
                    nv = bisect.bisect_right(kc, item[2])
                    out.append(("w", item[1], nv))
                else:
                    out.append(item)
            P.ops[e] = out

        def replay(E, ename):
            for item in P.ops[ename]:
                if item[0] == "d":
                    E.drain()
                    continue
                if item[0] == "w":
                    E.wait_ge(sems[item[1]], item[2])
                else:
                    _, build, tag = item
                    ins = build(E)
                    if tag is not None:
                        kind, s_ = tag
                        ins.then_inc(sems[s_], 16 if kind == "dma" else 1)

        with nc.Block() as blk:
            blk.tensor(lambda E: replay(E, "pe"))
            blk.vector(lambda E: replay(E, "dve"))
            blk.scalar(lambda E: replay(E, "act"))
            blk.sync(lambda E: replay(E, "sp"))
            blk.gpsimd(lambda E: replay(E, "pool"))

    return nc


def _host_inputs(inputs, Sx=S):
    x = np.ascontiguousarray(np.asarray(inputs["L0_attn_x"], dtype=np.float32))
    wq = np.asarray(inputs["L0_attn_Wq"], np.float32)
    wk = np.asarray(inputs["L0_attn_Wk"], np.float32)
    wv = np.asarray(inputs["L0_attn_Wv"], np.float32)
    wo = np.asarray(inputs["L0_attn_Wo"], np.float32)
    bk = np.asarray(inputs["L0_attn_bk"], np.float32)
    bq = np.asarray(inputs["L0_attn_bq"], np.float32)
    bv = np.asarray(inputs["L0_attn_bv"], np.float32)
    bo = np.asarray(inputs["L0_attn_bo"], np.float32)
    wg = np.asarray(inputs["L0_ffn_w_gate"], np.int32)
    wu = np.asarray(inputs["L0_ffn_w_up"], np.int32)
    wd = np.asarray(inputs["L0_ffn_w_down"], np.int32)

    wqT = np.ascontiguousarray(wq.T)
    wkT = np.ascontiguousarray(wk.T)
    wvT = np.ascontiguousarray(wv.T)
    woT = np.ascontiguousarray(wo.T)
    wgb = np.ascontiguousarray(wg.T.reshape(ND, 128, NJ, 128).transpose(2, 0, 1, 3))
    wub = np.ascontiguousarray(wu.T.reshape(ND, 128, NJ, 128).transpose(2, 0, 1, 3))
    wdb = np.ascontiguousarray(wd.T.reshape(NJ, 128, D))
    bkc = np.ascontiguousarray(bk.reshape(M, 1))
    bqc = np.ascontiguousarray(bq.reshape(M, 1))
    bvc = np.ascontiguousarray(bv.reshape(M, 1))
    bor = np.ascontiguousarray(bo.reshape(1, D))
    onesr = np.ones((1, 128), np.float32)

    lam_i = (DECAY ** np.arange(C)).astype(np.float32)
    lamqh = np.ascontiguousarray(np.broadcast_to(np.tile(lam_i, GPB)[None, :], (128, GPB * C)).astype(np.float32))
    wmth = np.zeros((C, C), np.float32)
    srow = (LR * DECAY ** (-1.0 - np.arange(C))).astype(np.float32)
    for s_ in range(C):
        wmth[s_, s_ + 1:] = srow[s_]
    lamkh = np.ascontiguousarray((LR * DECAY ** (C - 1 - np.arange(C))).reshape(C, 1).astype(np.float32))
    identh = np.eye(128, dtype=np.float32)

    shared = dict(wqT=wqT, wkT=wkT, wvT=wvT, woT=woT, bkc=bkc, bqc=bqc, bvc=bvc,
                  bor=bor, onesr=onesr, wgb=wgb, wub=wub, wdb=wdb, identd=identh,
                  lamqd=lamqh, wmtd=wmth, lamkd=lamkh)
    nb = x.shape[0]
    ncores = nb // BPC
    in_maps = []
    for i in range(ncores):
        m = dict(shared)
        m["x"] = np.ascontiguousarray(x[i * BPC:(i + 1) * BPC, :Sx])
        in_maps.append(m)
    return in_maps


def kernel(**inputs):
    in_maps = _host_inputs(inputs)
    nc = _build(S // C)
    res = run_bass_kernel_spmd(nc, in_maps, core_ids=list(range(NCORES)))
    out = np.concatenate([r["out"] for r in res.results], axis=0)
    return out.astype(np.float32)


# version 41
# speedup vs baseline: 1.0012x; 1.0012x over previous
import numpy as np
from collections import deque
from contextlib import ExitStack
import concourse.bass as bass
import concourse.mybir as mybir
from concourse.bass_utils import run_bass_kernel_spmd

F32 = mybir.dt.float32
BF16 = mybir.dt.bfloat16
I32 = mybir.dt.int32
AF = mybir.ActivationFunctionType
ALU = mybir.AluOpType

B, S, D, M, H = 32, 2048, 1024, 128, 3072
NCORES = 8
BPC = B // NCORES
GPB = 2
NGRP = BPC // GPB
C = 128
DECAY, LR = 0.99, 0.01
NJ = H // 128
ND = D // 128
MAGIC = float(0x5F3759DF)
LAMC = float(DECAY ** C)
EPS4 = float(1e-5 * 127.0 ** 4)

ENGS = ("pe", "dve", "act", "sp", "pool")
NSLOT = 8


class Plan:
    def __init__(self):
        self.sem_names = list(ENGS) + [f"{q}{i}" for q in ("sp", "pool") for i in range(NSLOT)]
        self.ops = {e: [] for e in ENGS}
        self.cnt = {s: 0 for s in self.sem_names}
        self.seen = {e: {s: 0 for s in self.sem_names} for e in ENGS}
        self.opno = {e: 0 for e in ENGS}
        self.drained = {e: 0 for e in ENGS}
        self.wr = {}
        self.rd = {}
        self.pend = {e: ([], []) for e in ENGS}
        self.dma_idx = {"sp": 0, "pool": 0}

    def _wait(self, eng, sem, val, wopno=None):
        if sem == eng and sem in ENGS and wopno is not None:
            if eng == "pe":
                return
            if self.drained[eng] >= wopno:
                return
            self.ops[eng].append(("d",))
            self.drained[eng] = self.opno[eng]
            return
        if val <= self.seen[eng][sem]:
            return
        self.seen[eng][sem] = val
        self.ops[eng].append(("w", sem, val))

    def _haz(self, eng, reads, writes):
        for k in reads:
            w = self.wr.get(k)
            if w is not None:
                self._wait(eng, w[0], w[1], w[2] if len(w) > 2 else None)
        for k in writes:
            w = self.wr.get(k)
            if w is not None:
                self._wait(eng, w[0], w[1], w[2] if len(w) > 2 else None)
            for sem, vv in self.rd.get(k, {}).items():
                if isinstance(vv, tuple):
                    self._wait(eng, sem, vv[0], vv[1])
                else:
                    self._wait(eng, sem, vv)

    def emit(self, eng, build, reads=(), writes=(), inc=1):
        if eng in ("sp", "pool"):
            self._haz(eng, reads, writes)
            i = self.dma_idx[eng]
            self.dma_idx[eng] = i + 1
            slot = f"{eng}{i % NSLOT}"
            self._wait(eng, slot, 16 * (i // NSLOT))
            self.cnt[slot] += 16
            val = self.cnt[slot]
            self.ops[eng].append(("o", build, ("dma", slot)))
            for k in reads:
                self.rd.setdefault(k, {})[slot] = val
            for k in writes:
                self.wr[k] = (slot, val)
                self.rd[k] = {}
            return
        self._haz(eng, reads, writes)
        self.opno[eng] += 1
        if inc == 0:
            pr, pw = self.pend[eng]
            pr.extend(reads)
            pw.extend(writes)
            self.ops[eng].append(("o", build, None))
            return
        self.cnt[eng] += inc
        n = self.cnt[eng]
        o = self.opno[eng]
        self.ops[eng].append(("o", build, ("eng", eng)))
        pr, pw = self.pend[eng]
        for k in list(reads) + pr:
            self.rd.setdefault(k, {})[eng] = (n, o)
        for k in list(writes) + pw:
            self.wr[k] = (eng, n, o)
            self.rd[k] = {}
        self.pend[eng] = ([], [])


def _build(nch=S // C):
    Sx = nch * C
    nc = bass.Bass("TRN2", target_bir_lowering=False)

    xd = nc.dram_tensor("x", [BPC, Sx, D], F32, kind="ExternalInput")
    wqT = nc.dram_tensor("wqT", [D, M], F32, kind="ExternalInput")
    wkT = nc.dram_tensor("wkT", [D, M], F32, kind="ExternalInput")
    wvT = nc.dram_tensor("wvT", [D, M], F32, kind="ExternalInput")
    woT = nc.dram_tensor("woT", [M, D], F32, kind="ExternalInput")
    bkc = nc.dram_tensor("bkc", [M, 1], F32, kind="ExternalInput")
    bqc = nc.dram_tensor("bqc", [M, 1], F32, kind="ExternalInput")
    bvc = nc.dram_tensor("bvc", [M, 1], F32, kind="ExternalInput")
    bor = nc.dram_tensor("bor", [1, D], F32, kind="ExternalInput")
    onesr = nc.dram_tensor("onesr", [1, 128], F32, kind="ExternalInput")
    wgb = nc.dram_tensor("wgb", [NJ, ND, 128, 128], I32, kind="ExternalInput")
    wub = nc.dram_tensor("wub", [NJ, ND, 128, 128], I32, kind="ExternalInput")
    wdb = nc.dram_tensor("wdb", [NJ, 128, D], I32, kind="ExternalInput")
    identd = nc.dram_tensor("identd", [128, 128], F32, kind="ExternalInput")
    lamqd = nc.dram_tensor("lamqd", [128, GPB * C], F32, kind="ExternalInput")
    wmtd = nc.dram_tensor("wmtd", [128, 128], F32, kind="ExternalInput")
    lamkd = nc.dram_tensor("lamkd", [128, 1], F32, kind="ExternalInput")
    outd = nc.dram_tensor("out", [BPC, Sx, D], F32, kind="ExternalOutput")

    es = ExitStack()
    sbuf = lambda nm, sh, dt: es.enter_context(nc.sbuf_tensor(nm, sh, dt))
    psum = lambda nm, sh, dt: es.enter_context(nc.psum_tensor(nm, sh, dt))

    with es:
        wg_sb = sbuf("wg_sb", [128, NJ * 1024], BF16)
        wu_sb = sbuf("wu_sb", [128, NJ * 1024], BF16)
        wd_sb = sbuf("wd_sb", [128, NJ * 1024], BF16)
        wq_sb = sbuf("wq_sb", [128, ND * 128], BF16)
        wk_sb = sbuf("wk_sb", [128, ND * 128], BF16)
        wv_sb = sbuf("wv_sb", [128, ND * 128], BF16)
        wo_sb = sbuf("wo_sb", [128, D], BF16)
        X = [sbuf("X0", [128, GPB * D], F32), sbuf("X1", [128, GPB * D], F32),
             sbuf("X2", [128, GPB * D], F32)]
        xhat = [sbuf("xhat0", [128, D], BF16), sbuf("xhat1", [128, D], BF16)]
        x1T = sbuf("x1T", [128, ND * GPB * C], BF16)
        x3T = sbuf("x3T", [128, ND * GPB * C], BF16)
        qt = sbuf("qt", [128, GPB * C], BF16)
        kt = sbuf("kt", [128, GPB * C], BF16)
        vt = sbuf("vt", [128, GPB * C], BF16)
        khat = sbuf("khat", [128, GPB * C], BF16)
        vtm = sbuf("vtm", [128, GPB * C], BF16)
        pwt = sbuf("pwt", [128, GPB * C], BF16)
        ot = sbuf("ot", [128, GPB * C], BF16)
        stf = sbuf("stf", [128, BPC * C], F32)
        stb = sbuf("stb", [128, BPC * C], BF16)
        ytA = sbuf("ytA", [128, D], BF16)
        ytB = sbuf("ytB", [128, D], BF16)
        hT = sbuf("hT", [128, 2 * GPB * C], BF16)
        rowsm = sbuf("rowsm", [128, D], BF16)
        ones_sb = sbuf("ones_sb", [1, 128], BF16)
        identb = sbuf("identb", [128, 128], BF16)
        lamq = sbuf("lamq", [128, GPB * C], BF16)
        wmt = sbuf("wmt", [128, 128], BF16)
        lamk = sbuf("lamk", [128, 1], F32)
        bk_s = sbuf("bk_s", [M, 1], F32)
        bq_s = sbuf("bq_s", [M, 1], F32)
        bv_s = sbuf("bv_s", [M, 1], F32)
        _bn6A = sbuf("bn6_A", [128, 12 * GPB], F32)
        _bn6B = sbuf("bn6_B", [128, 12 * GPB], F32)
        bn6 = [_bn6A, _bn6B, _bn6A, _bn6B]
        agg = [sbuf(f"agg_{i}", [128, 2 * GPB], F32) for i in range(4)]
        uu = [sbuf(f"uu_{i}", [128, GPB], F32) for i in range(4)]
        tt = [sbuf(f"tt_{i}", [128, GPB], F32) for i in range(4)]
        ui = [sbuf(f"ui_{i}", [128, GPB], I32) for i in range(4)]
        rr = [sbuf(f"rr_{i}", [128, GPB], F32) for i in range(4)]
        mrc = [sbuf(f"mrc_{i}", [128, GPB], F32) for i in range(4)]
        stx = [sbuf(f"stx_{i}", [128, 4 * GPB], F32) for i in (0, 2)]
        stxm = {0: 0, 2: 1}

        bank = [psum(f"pb{i}", [128, 512], F32) for i in range(8)]
        bankbf = [b[:].bitcast(BF16) for b in bank]

        P = Plan()
        sems = {}
        for s in P.sem_names:
            sems[s] = es.enter_context(nc.semaphore(f"sem_{s}"))

        def mm(build, reads, wbank, inc):
            P.emit("pe", build, reads=reads, writes=(("bank", wbank),), inc=inc)

        P.emit("pool", lambda E: E.dma_start(wq_sb[:], wqT[:, :].rearrange("(dc p) m -> p dc m", p=128)), writes=(("wq",),))
        P.emit("pool", lambda E: E.dma_start(wk_sb[:], wkT[:, :].rearrange("(dc p) m -> p dc m", p=128)), writes=(("wk",),))
        P.emit("pool", lambda E: E.dma_start(wv_sb[:], wvT[:, :].rearrange("(dc p) m -> p dc m", p=128)), writes=(("wv",),))
        P.emit("pool", lambda E: E.dma_start(wo_sb[:], woT[:, :]), writes=(("wo",),))
        P.emit("pool", lambda E: E.dma_start(identb[:], identd[:, :]), writes=(("ident",),))
        P.emit("pool", lambda E: E.dma_start(lamq[:], lamqd[:, :]), writes=(("lamq",),))
        P.emit("pool", lambda E: E.dma_start(wmt[:], wmtd[:, :]), writes=(("wmt",),))
        P.emit("pool", lambda E: E.dma_start(rowsm[0:1, :], bor[:, :]), writes=(("bo",),))
        P.emit("pool", lambda E: E.dma_start(ones_sb[:, :], onesr[:, :]), writes=(("ones",),))
        P.emit("sp", lambda E: E.dma_start(lamk[:], lamkd[:, :]), writes=(("lamk",),))
        P.emit("sp", lambda E: E.dma_start(bk_s[:], bkc[:, :]), writes=(("bk",),))
        P.emit("sp", lambda E: E.dma_start(bq_s[:], bqc[:, :]), writes=(("bq",),))
        P.emit("sp", lambda E: E.dma_start(bv_s[:], bvc[:, :]), writes=(("bv",),))
        P.emit("dve", lambda E: E.memset(stf[:], 0.0), writes=tuple(("stf", b) for b in range(BPC)))
        P.emit("dve", lambda E: E.memset(stb[:], 0.0), writes=tuple(("stb", b) for b in range(BPC)))

        for j in range(NJ):
            src_g = wgb[j].transpose([1, 0, 2])
            src_u = wub[j].transpose([1, 0, 2])
            P.emit("pool", lambda E, s=src_g, j=j: E.dma_start(
                wg_sb[:, j * 1024:(j + 1) * 1024].rearrange("p (dc h) -> p dc h", dc=ND), s), writes=(("wg", j),))
            P.emit("pool", lambda E, s=src_u, j=j: E.dma_start(
                wu_sb[:, j * 1024:(j + 1) * 1024].rearrange("p (dc h) -> p dc h", dc=ND), s), writes=(("wu", j),))
            P.emit("pool", lambda E, j=j: E.dma_start(wd_sb[:, j * 1024:(j + 1) * 1024], wdb[j]), writes=(("wd", j),))

        def rsqrt_step_list(pt, cols, eps):
            u, t_, i_, r_ = uu[pt], tt[pt], ui[pt], rr[pt]
            lo, hi = cols.start, cols.stop
            ku, kt_, ki = ("uu", pt, lo), ("tt", pt, lo), ("ui", pt, lo)
            vsrc = agg[pt][:, 2 * lo + 1: 2 * hi: 2]
            vkeys = tuple(("agg", pt, b) for b in range(lo, hi))
            cs = slice(lo, hi)
            y = i_[:, cs].bitcast(F32)
            steps = [
                lambda: P.emit("dve", lambda E: E.tensor_scalar(u[:, cs], vsrc, eps, None, ALU.add), reads=vkeys, writes=(ku,)),
                lambda: P.emit("dve", lambda E: E.tensor_copy(t_[:, cs], u[:, cs].bitcast(I32)), reads=(ku,), writes=(kt_,)),
                lambda: P.emit("dve", lambda E: E.tensor_scalar(t_[:, cs], t_[:, cs], -0.5, MAGIC, ALU.mult, ALU.add), reads=(kt_,), writes=(kt_,)),
                lambda: P.emit("dve", lambda E: E.tensor_copy(i_[:, cs], t_[:, cs]), reads=(kt_,), writes=(ki,)),
            ]
            for _ in range(3):
                steps += [
                    lambda: P.emit("dve", lambda E: E.tensor_mul(t_[:, cs], y, y), reads=(ki,), writes=(kt_,)),
                    lambda: P.emit("dve", lambda E: E.tensor_mul(t_[:, cs], t_[:, cs], u[:, cs]), reads=(kt_, ku), writes=(kt_,)),
                    lambda: P.emit("dve", lambda E: E.tensor_scalar(t_[:, cs], t_[:, cs], -0.5, 1.5, ALU.mult, ALU.add), reads=(kt_,), writes=(kt_,)),
                    lambda: P.emit("dve", lambda E: E.tensor_mul(y, y, t_[:, cs]), reads=(ki, kt_), writes=(ki,)),
                ]
            steps.append(lambda: P.emit("dve", lambda E: E.tensor_copy(r_[:, cs], y), reads=(ki,),
                                        writes=tuple(("rr", pt, b) for b in range(lo, hi))))
            return steps

        def bn_from(pt, lb, src_halves, src_keys):
            steps = []
            for hh in (0, 1):
                steps.append(lambda hh=hh: P.emit("dve", lambda E: E.bn_stats(
                    bn6[pt][:, lb * 12 + hh * 6: lb * 12 + hh * 6 + 6], src_halves[hh]),
                    reads=src_keys[hh], writes=(("bn6", pt, lb, hh),)))
            steps.append(lambda: P.emit("dve", lambda E: E.bn_aggr(
                agg[pt][:, 2 * lb:2 * lb + 2], bn6[pt][:, lb * 12: lb * 12 + 12]),
                reads=(("bn6", pt, lb, 0), ("bn6", pt, lb, 1)), writes=(("agg", pt, lb),)))
            return steps

        def stats_act(pt, lb, src_halves, src_keys, trash_ap, trash_key):
            sx = stx[stxm[pt]]
            steps = []
            for hh in (0, 1):
                steps.append(lambda hh=hh: P.emit("act", lambda E: E.activation(
                    trash_ap, src_halves[hh], AF.Identity,
                    accum_out=sx[:, lb * 4 + hh: lb * 4 + hh + 1]),
                    reads=src_keys[hh], writes=(trash_key, ("x1T", 0, 1), ("sx", pt, lb, hh))))
                steps.append(lambda hh=hh: P.emit("act", lambda E: E.activation(
                    trash_ap, src_halves[hh], AF.Square,
                    accum_out=sx[:, lb * 4 + 2 + hh: lb * 4 + 3 + hh]),
                    reads=src_keys[hh], writes=(trash_key, ("x1T", 0, 1), ("sx", pt, lb, 2 + hh))))
            tcol = tt[pt][:, lb:lb + 1]
            mcol = agg[pt][:, 2 * lb:2 * lb + 1]
            vcol = agg[pt][:, 2 * lb + 1:2 * lb + 2]
            mmcol = mrc[pt][:, lb:lb + 1]
            steps.append(lambda: P.emit("dve", lambda E: E.tensor_add(
                tcol, sx[:, lb * 4:lb * 4 + 1], sx[:, lb * 4 + 1:lb * 4 + 2]),
                reads=(("sx", pt, lb, 0), ("sx", pt, lb, 1)), writes=(("tt", pt, lb),)))
            steps.append(lambda: P.emit("dve", lambda E: E.tensor_scalar(
                mcol, tcol, 1.0 / D, None, ALU.mult),
                reads=(("tt", pt, lb),), writes=(("aggm", pt, lb),)))
            steps.append(lambda: P.emit("dve", lambda E: E.tensor_add(
                tcol, sx[:, lb * 4 + 2:lb * 4 + 3], sx[:, lb * 4 + 3:lb * 4 + 4]),
                reads=(("sx", pt, lb, 2), ("sx", pt, lb, 3)), writes=(("tt", pt, lb),)))
            steps.append(lambda: P.emit("dve", lambda E: E.tensor_mul(mmcol, mcol, mcol),
                reads=(("aggm", pt, lb),), writes=(("mrc", pt, lb),)))
            steps.append(lambda: P.emit("dve", lambda E: E.scalar_tensor_tensor(
                vcol, tcol, 1.0 / D, mmcol, ALU.mult, ALU.subtract),
                reads=(("tt", pt, lb), ("mrc", pt, lb)), writes=(("agg", pt, lb),)))
            return steps

        def mr_step(pt, lb):
            return lambda: P.emit("dve", lambda E: E.tensor_mul(
                mrc[pt][:, lb:lb + 1], agg[pt][:, 2 * lb:2 * lb + 1], rr[pt][:, lb:lb + 1]),
                reads=(("agg", pt, lb), ("aggm", pt, lb), ("rr", pt, lb)), writes=(("mrc", pt, lb),))

        def build_A(ti, g, cidx):
            par = ti % 3
            Xc = X[par]
            A = []

            def xk(lb):
                return ("X", par, lb)

            Adma = []
            A.append(("EPI_BARRIER", ti - 3))
            for lb in range(GPB):
                gb = 2 * g + lb
                A.append(lambda lb=lb, gb=gb: P.emit(
                    "sp", lambda E: E.dma_start(Xc[:, lb * D:(lb + 1) * D], xd[gb, cidx * C:(cidx + 1) * C, :]),
                    writes=(xk(lb),)))
            for lb in range(GPB):
                A += stats_act(0, lb, [Xc[:, lb * D:lb * D + 512], Xc[:, lb * D + 512:lb * D + 1024]],
                               [(xk(lb),), (xk(lb),)], x1T[:, 0:512], ("x1T", 0, 0))
            A += rsqrt_step_list(0, slice(0, GPB), 1e-5)
            A.append(("PRO_BARRIER", ti - 1))
            for lb in range(GPB):
                A.append(lambda lb=lb: P.emit("dve", lambda E: E.tensor_scalar(
                    xhat[lb][:], Xc[:, lb * D:(lb + 1) * D],
                    agg[0][:, 2 * lb:2 * lb + 1], rr[0][:, lb:lb + 1], ALU.subtract, ALU.mult),
                    reads=(xk(lb), ("agg", 0, lb), ("aggm", 0, lb), ("rr", 0, lb)), writes=(("xhat", lb),)))
            for lb in range(GPB):
                bk_ = 4 + lb

                def trs(lb=lb, bk_=bk_):
                    for dc in range(ND):
                        P.emit("pe", lambda E, dc=dc, lb=lb, bk_=bk_: E.transpose(
                            bankbf[bk_][:, dc * 128:(dc + 1) * 128], xhat[lb][:, dc * 128:(dc + 1) * 128], identb[:]),
                            reads=(("xhat", lb), ("ident",)), writes=(("bank", bk_),), inc=1 if dc == ND - 1 else 0)
                A.append(trs)
                A.append(lambda lb=lb, bk_=bk_: P.emit("dve", lambda E: E.tensor_copy(
                    x1T[:].rearrange("p (dc bt) -> p dc bt", dc=ND)[:, :, lb * 128:(lb + 1) * 128],
                    bankbf[bk_].rearrange("p (dc t) -> p dc t", dc=ND)),
                    reads=(("bank", bk_),), writes=tuple(("x1T", dc, lb) for dc in range(ND))))
            x1keys = tuple(("x1T", dc, lb) for dc in range(ND) for lb in range(GPB))

            def proj(wsb, wkey, bnk):
                def f():
                    for dc in range(ND):
                        mm(lambda E, dc=dc: E.matmul(
                            bank[bnk][:, 0:GPB * C], wsb[:, dc * 128:(dc + 1) * 128],
                            x1T[:, dc * GPB * C:(dc + 1) * GPB * C],
                            start=dc == 0, stop=dc == ND - 1, skip_group_check=True),
                            reads=(wkey,) + x1keys, wbank=bnk, inc=1 if dc == ND - 1 else 0)
                return f

            A.append(proj(wq_sb, ("wq",), 4))
            A.append(lambda: P.emit("act", lambda E: E.activation(
                qt[:], bank[4][:, 0:GPB * C], AF.Identity, bias=bq_s[:]),
                reads=(("bank", 4), ("bq",)), writes=(("qt",),)))
            A.append(lambda: P.emit("dve", lambda E: E.tensor_tensor(
                qt[:], qt[:], lamq[:], ALU.mult),
                reads=(("qt",), ("lamq",)), writes=(("qt",),)))
            A.append(proj(wk_sb, ("wk",), 5))
            A.append(lambda: P.emit("act", lambda E: E.activation(
                kt[:], bank[5][:, 0:GPB * C], AF.Identity, bias=bk_s[:]),
                reads=(("bank", 5), ("bk",)), writes=(("kt",),)))
            A.append(proj(wv_sb, ("wv",), 4))
            A.append(lambda: P.emit("act", lambda E: E.activation(
                vt[:], bank[4][:, 0:GPB * C], AF.Identity, bias=bv_s[:]),
                reads=(("bank", 4), ("bv",)), writes=(("vt",),)))
            for lb in range(GPB):
                A.append(lambda lb=lb: P.emit("pe", lambda E: E.transpose(
                    bankbf[5][:, lb * 128:(lb + 1) * 128], kt[:, lb * 128:(lb + 1) * 128], identb[:]),
                    reads=(("kt",), ("ident",)), writes=(("bank", 5),)))
                A.append(lambda lb=lb: P.emit("pe", lambda E: E.transpose(
                    bankbf[4][:, lb * 128:(lb + 1) * 128], vt[:, lb * 128:(lb + 1) * 128], identb[:]),
                    reads=(("vt",), ("ident",)), writes=(("bank", 4),)))
            for lb in range(GPB):
                A.append(lambda lb=lb: P.emit("dve", lambda E: E.tensor_scalar(
                    khat[:, lb * 128:(lb + 1) * 128], bankbf[5][:, lb * 128:(lb + 1) * 128], lamk[:], None, ALU.mult),
                    reads=(("bank", 5), ("lamk",)), writes=(("khat", lb),)))
                A.append(lambda lb=lb: P.emit("act", lambda E: E.activation(
                    vtm[:, lb * 128:(lb + 1) * 128], bankbf[4][:, lb * 128:(lb + 1) * 128], AF.Identity),
                    reads=(("bank", 4),), writes=(("vtm", lb),)))
            for lb in range(GPB):
                gb = 2 * g + lb
                bs = slice(lb * 128, (lb + 1) * 128)
                A.append(lambda bs=bs: P.emit("pe", lambda E: E.matmul(
                    bank[5][:, 0:128], kt[:, bs], qt[:, bs], start=True, stop=True, skip_group_check=True),
                    reads=(("kt",), ("qt",)), writes=(("bank", 5),)))
                A.append(lambda lb=lb, bs=bs: P.emit("dve", lambda E: E.tensor_tensor(
                    pwt[:, bs], bank[5][:, 0:128], wmt[:], ALU.mult),
                    reads=(("bank", 5), ("wmt",)), writes=(("pwt", lb),)))

                def omm(lb=lb, gb=gb, bs=bs):
                    gs = slice(gb * 128, (gb + 1) * 128)
                    mm(lambda E: E.matmul(bank[4][:, 0:128], vtm[:, bs], pwt[:, bs],
                                          start=True, stop=False, skip_group_check=True),
                       reads=(("vtm", lb), ("pwt", lb)), wbank=4, inc=0)
                    mm(lambda E: E.matmul(bank[4][:, 0:128], stb[:, gs], qt[:, bs],
                                          start=False, stop=True, skip_group_check=True),
                       reads=(("stb", gb), ("qt",)), wbank=4, inc=1)
                A.append(omm)
                A.append(lambda lb=lb, bs=bs: P.emit("act", lambda E: E.activation(
                    ot[:, bs], bank[4][:, 0:128], AF.Identity), reads=(("bank", 4),), writes=(("ot", lb),)))
                A.append(lambda lb=lb, bs=bs: P.emit("pe", lambda E: E.matmul(
                    bank[5][:, 128:256], khat[:, bs], vtm[:, bs], start=True, stop=True, skip_group_check=True),
                    reads=(("khat", lb), ("vtm", lb)), writes=(("bank", 5),)))

                def stup(gb=gb):
                    gs = slice(gb * 128, (gb + 1) * 128)
                    P.emit("dve", lambda E: E.tensor_scalar(stf[:, gs], stf[:, gs], LAMC, None, ALU.mult),
                           reads=(("stf", gb),), writes=(("stf", gb),))
                    P.emit("dve", lambda E: E.tensor_tensor(stf[:, gs], stf[:, gs], bank[5][:, 128:256], ALU.add),
                           reads=(("stf", gb), ("bank", 5)), writes=(("stf", gb),))
                    P.emit("dve", lambda E: E.tensor_copy(stb[:, gs], stf[:, gs]),
                           reads=(("stf", gb),), writes=(("stb", gb),))
                A.append(stup)
            for lb in range(GPB):
                bs = slice(lb * 128, (lb + 1) * 128)

                def y1mm(lb=lb, bs=bs):
                    for dh in (0, 1):
                        mm(lambda E, dh=dh: E.matmul(bank[4 + dh][:], ot[:, bs],
                                                     wo_sb[:, dh * 512:(dh + 1) * 512],
                                                     start=True, stop=False, skip_group_check=True),
                           reads=(("ot", lb), ("wo",)), wbank=4 + dh, inc=0)
                        mm(lambda E, dh=dh: E.matmul(bank[4 + dh][:], ones_sb[0:1, :],
                                                     rowsm[0:1, dh * 512:(dh + 1) * 512],
                                                     start=False, stop=True, skip_group_check=True),
                           reads=(("bo",), ("ones",)), wbank=4 + dh, inc=1)
                A.append(y1mm)
                A += bn_from(1, lb, [bank[4][:], bank[5][:]], [(("bank", 4),), (("bank", 5),)])
                A += rsqrt_step_list(1, slice(lb, lb + 1), 1e-5)
                A.append(mr_step(1, lb))

                def apply2(lb=lb):
                    P.emit("dve", lambda E: E.tensor_scalar(
                        Xc[:, lb * D:(lb + 1) * D], Xc[:, lb * D:(lb + 1) * D],
                        mrc[1][:, lb:lb + 1], None, ALU.subtract),
                        reads=(xk(lb), ("mrc", 1, lb)), writes=(xk(lb),))
                    for dh in (0, 1):
                        P.emit("dve", lambda E, dh=dh: E.scalar_tensor_tensor(
                            Xc[:, lb * D + dh * 512: lb * D + (dh + 1) * 512],
                            bank[4 + dh][:], rr[1][:, lb:lb + 1],
                            Xc[:, lb * D + dh * 512: lb * D + (dh + 1) * 512],
                            ALU.mult, ALU.add),
                            reads=(("bank", 4 + dh), ("rr", 1, lb), (xk(lb),)), writes=(xk(lb),))
                A.append(apply2)
                A += bn_from(2, lb, [Xc[:, lb * D:lb * D + 512], Xc[:, lb * D + 512:lb * D + 1024]],
                             [(xk(lb),), (xk(lb),)])
            A += rsqrt_step_list(2, slice(0, GPB), 1e-5)
            for lb in range(GPB):
                A.append(lambda lb=lb: P.emit("dve", lambda E: E.tensor_scalar(
                    xhat[lb][:], Xc[:, lb * D:(lb + 1) * D],
                    agg[2][:, 2 * lb:2 * lb + 1], rr[2][:, lb:lb + 1], ALU.subtract, ALU.mult),
                    reads=(xk(lb), ("agg", 2, lb), ("aggm", 2, lb), ("rr", 2, lb)),
                    writes=(("xhat", lb),)))
            A.append(("TILE_DONE", ti))
            return A

        EPI_BANKS = deque()
        EPI_REST = deque()
        AQ = deque()

        a_done = set()
        epi_done = set()
        pro_done = set()

        def drain(q, k):
            while q and k > 0:
                h = q[0]
                if isinstance(h, tuple) and h[0] == "EPI_BARRIER":
                    if h[1] >= 0 and h[1] not in epi_done:
                        return
                    q.popleft()
                    continue
                if isinstance(h, tuple) and h[0] == "TILE_DONE":
                    a_done.add(h[1])
                    q.popleft()
                    continue
                if isinstance(h, tuple) and h[0] == "EPI_DONE":
                    epi_done.add(h[1])
                    q.popleft()
                    continue
                if isinstance(h, tuple) and h[0] == "PRO_BARRIER":
                    if h[1] >= 0 and h[1] not in pro_done:
                        return
                    q.popleft()
                    continue
                q.popleft()()
                k -= 1

        def drain_until_tile(ti):
            while ti not in a_done:
                h = AQ[0]
                if isinstance(h, tuple) and h[0] == "PRO_BARRIER":
                    assert h[1] < 0 or h[1] in pro_done, (h,)
                    AQ.popleft()
                    continue
                if isinstance(h, tuple) and h[0] == "EPI_BARRIER":
                    if h[1] >= 0 and h[1] not in epi_done:
                        drain(EPI_BANKS, 10 ** 9)
                        drain(EPI_REST, 10 ** 9)
                        assert h[1] < 0 or h[1] in epi_done, (h,)
                    AQ.popleft()
                    continue
                if isinstance(h, tuple) and h[0] == "TILE_DONE":
                    a_done.add(h[1])
                    AQ.popleft()
                    continue
                AQ.popleft()()

        def emit_F(ti, g, cidx):
            par = ti % 3
            Xc = X[par]

            def xk(lb):
                return ("X", par, lb)

            drain_until_tile(ti)
            drain(EPI_BANKS, 99)
            for lb in range(GPB):
                bk_ = 2 + lb
                for dc in range(ND):
                    P.emit("pe", lambda E, lb=lb, dc=dc, bk_=bk_: E.transpose(
                        bankbf[bk_][:, dc * 128:(dc + 1) * 128], xhat[lb][:, dc * 128:(dc + 1) * 128], identb[:]),
                        reads=(("xhat", lb), ("ident",)), writes=(("bank", bk_),), inc=1 if dc == ND - 1 else 0)
                P.emit("dve", lambda E, lb=lb, bk_=bk_: E.tensor_copy(
                    x3T[:].rearrange("p (dc bt) -> p dc bt", dc=ND)[:, :, lb * 128:(lb + 1) * 128],
                    bankbf[bk_].rearrange("p (dc t) -> p dc t", dc=ND)),
                    reads=(("bank", bk_),), writes=tuple(("x3T", dc, lb) for dc in range(ND)))
            x3keys = tuple(("x3T", dc, lb) for dc in range(ND) for lb in range(GPB))
            pro_done.add(ti)

            def emit_down(j):
                jp = j % 2
                for lb in range(GPB):
                    for dh in range(2):
                        dbk = (6, 7, 0, 1)[lb * 2 + dh]
                        mm(lambda E, j=j, jp=jp, lb=lb, dh=dh, dbk=dbk: E.matmul(
                            bank[dbk][:], hT[:, jp * 256 + lb * 128: jp * 256 + (lb + 1) * 128],
                            wd_sb[:, j * 1024 + dh * 512: j * 1024 + (dh + 1) * 512],
                            start=j == 0, stop=j == NJ - 1, skip_group_check=True),
                            reads=(("hT", jp), ("wd", j)), wbank=dbk,
                            inc=1 if j == NJ - 1 else 0)

            for j in range(NJ):
                jp = j % 2
                for dc in range(ND):
                    mm(lambda E, j=j, dc=dc: E.matmul(
                        bank[2][:, 0:256], wg_sb[:, j * 1024 + dc * 128: j * 1024 + (dc + 1) * 128],
                        x3T[:, dc * 256:(dc + 1) * 256], start=dc == 0, stop=dc == ND - 1,
                        skip_group_check=True),
                        reads=(("wg", j),) + x3keys, wbank=2, inc=1 if dc == ND - 1 else 0)
                P.emit("act", lambda E, jp=jp: E.activation(
                    hT[:, jp * 256:(jp + 1) * 256], bank[2][:, 0:256], AF.Silu, scale=1.0 / 127.0),
                    reads=(("bank", 2),), writes=(("hT", jp),))
                if j >= 1:
                    drain(AQ, 3)
                for dc in range(ND):
                    mm(lambda E, j=j, dc=dc: E.matmul(
                        bank[3][:, 0:256], wu_sb[:, j * 1024 + dc * 128: j * 1024 + (dc + 1) * 128],
                        x3T[:, dc * 256:(dc + 1) * 256], start=dc == 0, stop=dc == ND - 1,
                        skip_group_check=True),
                        reads=(("wu", j),) + x3keys, wbank=3, inc=1 if dc == ND - 1 else 0)
                if j > 0:
                    emit_down(j - 1)
                if j >= 1:
                    drain(AQ, 2)
                P.emit("dve", lambda E, jp=jp: E.tensor_tensor(
                    hT[:, jp * 256:(jp + 1) * 256], hT[:, jp * 256:(jp + 1) * 256], bank[3][:, 0:256], ALU.mult),
                    reads=(("hT", jp), ("bank", 3)), writes=(("hT", jp),))
                drain(EPI_REST, 4)
                if j >= 1:
                    drain(AQ, 2)
            emit_down(NJ - 1)

            yts = (ytA, ytB)
            for lb in range(GPB):
                for dh in range(2):
                    dbk = (6, 7, 0, 1)[lb * 2 + dh]
                    EPI_BANKS.append(lambda lb=lb, dh=dh, dbk=dbk: P.emit(
                        "act", lambda E: E.activation(yts[lb][:, dh * 512:(dh + 1) * 512], bank[dbk][:], AF.Identity),
                        reads=(("bank", dbk),), writes=(("yt", lb, dh),)))
            for lb in range(GPB):
                for st in bn_from(3, lb, [yts[lb][:, 0:512], yts[lb][:, 512:1024]],
                                  [(("yt", lb, 0),), (("yt", lb, 1),)]):
                    EPI_REST.append(st)
            for st in rsqrt_step_list(3, slice(0, GPB), EPS4):
                EPI_REST.append(st)
            for lb in range(GPB):
                EPI_REST.append(mr_step(3, lb))
            for lb in range(GPB):
                gb = 2 * g + lb

                def apply4(lb=lb):
                    for dh in (0, 1):
                        P.emit("dve", lambda E, dh=dh: E.tensor_scalar(
                            yts[lb][:, dh * 512:(dh + 1) * 512], yts[lb][:, dh * 512:(dh + 1) * 512],
                            rr[3][:, lb:lb + 1], mrc[3][:, lb:lb + 1], ALU.mult, ALU.subtract),
                            reads=(("yt", lb, dh), ("rr", 3, lb), ("mrc", 3, lb)), writes=(("yt", lb, dh),))
                        P.emit("dve", lambda E, dh=dh: E.tensor_tensor(
                            Xc[:, lb * D + dh * 512: lb * D + (dh + 1) * 512],
                            Xc[:, lb * D + dh * 512: lb * D + (dh + 1) * 512],
                            yts[lb][:, dh * 512:(dh + 1) * 512], ALU.add),
                            reads=(("yt", lb, dh), (xk(lb),)), writes=(xk(lb),))
                EPI_REST.append(apply4)
                EPI_REST.append(lambda lb=lb, gb=gb: P.emit(
                    "sp", lambda E: E.dma_start(outd[gb, cidx * C:(cidx + 1) * C, :], Xc[:, lb * D:(lb + 1) * D]),
                    reads=(xk(lb),)))
            EPI_REST.append(("EPI_DONE", ti))

        tiles = [(g, c) for c in range(nch) for g in range(NGRP)]
        AQ.extend(build_A(0, *tiles[0]))
        for ti, (g, c) in enumerate(tiles):
            if ti + 1 < len(tiles):
                AQ.extend(build_A(ti + 1, *tiles[ti + 1]))
            emit_F(ti, g, c)
        if tiles:
            drain_until_tile(len(tiles) - 1)
        while EPI_BANKS or EPI_REST or AQ:
            drain(EPI_BANKS, 10 ** 9)
            drain(EPI_REST, 10 ** 9)
            drain(AQ, 10 ** 9)

        for i in range(NSLOT):
            for q in ("sp", "pool"):
                s_ = f"{q}{i}"
                if P.cnt[s_]:
                    P.ops["sp"].append(("w", s_, P.cnt[s_]))

        waited = {s: set() for s in P.sem_names}
        for e in ENGS:
            for item in P.ops[e]:
                if item[0] == "w":
                    waited[item[1]].add(item[2])
        for e in ENGS:
            kept = []
            n = 0
            newops = []
            for item in P.ops[e]:
                if item[0] == "o" and item[2] is not None and item[2][0] == "eng":
                    n += 1
                    if n in waited[item[2][1]]:
                        kept.append(n)
                        newops.append(item)
                    else:
                        newops.append(("o", item[1], None))
                else:
                    newops.append(item)
            P.ops[e] = newops
            P.kept_counts = getattr(P, "kept_counts", {})
            P.kept_counts[e] = kept
        import bisect
        for e in ENGS:
            out = []
            for item in P.ops[e]:
                if item[0] == "w" and item[1] in ENGS:
                    kc = P.kept_counts[item[1]]
                    nv = bisect.bisect_right(kc, item[2])
                    out.append(("w", item[1], nv))
                else:
                    out.append(item)
            P.ops[e] = out

        def replay(E, ename):
            for item in P.ops[ename]:
                if item[0] == "d":
                    E.drain()
                    continue
                if item[0] == "w":
                    E.wait_ge(sems[item[1]], item[2])
                else:
                    _, build, tag = item
                    ins = build(E)
                    if tag is not None:
                        kind, s_ = tag
                        ins.then_inc(sems[s_], 16 if kind == "dma" else 1)

        with nc.Block() as blk:
            blk.tensor(lambda E: replay(E, "pe"))
            blk.vector(lambda E: replay(E, "dve"))
            blk.scalar(lambda E: replay(E, "act"))
            blk.sync(lambda E: replay(E, "sp"))
            blk.gpsimd(lambda E: replay(E, "pool"))

    return nc


def _host_inputs(inputs, Sx=S):
    x = np.ascontiguousarray(np.asarray(inputs["L0_attn_x"], dtype=np.float32))
    wq = np.asarray(inputs["L0_attn_Wq"], np.float32)
    wk = np.asarray(inputs["L0_attn_Wk"], np.float32)
    wv = np.asarray(inputs["L0_attn_Wv"], np.float32)
    wo = np.asarray(inputs["L0_attn_Wo"], np.float32)
    bk = np.asarray(inputs["L0_attn_bk"], np.float32)
    bq = np.asarray(inputs["L0_attn_bq"], np.float32)
    bv = np.asarray(inputs["L0_attn_bv"], np.float32)
    bo = np.asarray(inputs["L0_attn_bo"], np.float32)
    wg = np.asarray(inputs["L0_ffn_w_gate"], np.int32)
    wu = np.asarray(inputs["L0_ffn_w_up"], np.int32)
    wd = np.asarray(inputs["L0_ffn_w_down"], np.int32)

    wqT = np.ascontiguousarray(wq.T)
    wkT = np.ascontiguousarray(wk.T)
    wvT = np.ascontiguousarray(wv.T)
    woT = np.ascontiguousarray(wo.T)
    wgb = np.ascontiguousarray(wg.T.reshape(ND, 128, NJ, 128).transpose(2, 0, 1, 3))
    wub = np.ascontiguousarray(wu.T.reshape(ND, 128, NJ, 128).transpose(2, 0, 1, 3))
    wdb = np.ascontiguousarray(wd.T.reshape(NJ, 128, D))
    bkc = np.ascontiguousarray(bk.reshape(M, 1))
    bqc = np.ascontiguousarray(bq.reshape(M, 1))
    bvc = np.ascontiguousarray(bv.reshape(M, 1))
    bor = np.ascontiguousarray(bo.reshape(1, D))
    onesr = np.ones((1, 128), np.float32)

    lam_i = (DECAY ** np.arange(C)).astype(np.float32)
    lamqh = np.ascontiguousarray(np.broadcast_to(np.tile(lam_i, GPB)[None, :], (128, GPB * C)).astype(np.float32))
    wmth = np.zeros((C, C), np.float32)
    srow = (LR * DECAY ** (-1.0 - np.arange(C))).astype(np.float32)
    for s_ in range(C):
        wmth[s_, s_ + 1:] = srow[s_]
    lamkh = np.ascontiguousarray((LR * DECAY ** (C - 1 - np.arange(C))).reshape(C, 1).astype(np.float32))
    identh = np.eye(128, dtype=np.float32)

    shared = dict(wqT=wqT, wkT=wkT, wvT=wvT, woT=woT, bkc=bkc, bqc=bqc, bvc=bvc,
                  bor=bor, onesr=onesr, wgb=wgb, wub=wub, wdb=wdb, identd=identh,
                  lamqd=lamqh, wmtd=wmth, lamkd=lamkh)
    nb = x.shape[0]
    ncores = nb // BPC
    in_maps = []
    for i in range(ncores):
        m = dict(shared)
        m["x"] = np.ascontiguousarray(x[i * BPC:(i + 1) * BPC, :Sx])
        in_maps.append(m)
    return in_maps


def kernel(**inputs):
    in_maps = _host_inputs(inputs)
    nc = _build(S // C)
    res = run_bass_kernel_spmd(nc, in_maps, core_ids=list(range(NCORES)))
    out = np.concatenate([r["out"] for r in res.results], axis=0)
    return out.astype(np.float32)


# version 42
# speedup vs baseline: 1.0322x; 1.0309x over previous
import numpy as np
from collections import deque
from contextlib import ExitStack
import concourse.bass as bass
import concourse.mybir as mybir
from concourse.bass_utils import run_bass_kernel_spmd

F32 = mybir.dt.float32
BF16 = mybir.dt.bfloat16
I32 = mybir.dt.int32
AF = mybir.ActivationFunctionType
ALU = mybir.AluOpType

B, S, D, M, H = 32, 2048, 1024, 128, 3072
NCORES = 8
BPC = B // NCORES
GPB = 2
NGRP = BPC // GPB
C = 128
DECAY, LR = 0.99, 0.01
NJ = H // 128
ND = D // 128
MAGIC = float(0x5F3759DF)
LAMC = float(DECAY ** C)
EPS4 = float(1e-5 * 127.0 ** 4)

ENGS = ("pe", "dve", "act", "sp", "pool")
NSLOT = 8


class Plan:
    def __init__(self):
        self.sem_names = list(ENGS) + [f"{q}{i}" for q in ("sp", "pool") for i in range(NSLOT)]
        self.ops = {e: [] for e in ENGS}
        self.cnt = {s: 0 for s in self.sem_names}
        self.seen = {e: {s: 0 for s in self.sem_names} for e in ENGS}
        self.opno = {e: 0 for e in ENGS}
        self.drained = {e: 0 for e in ENGS}
        self.wr = {}
        self.rd = {}
        self.pend = {e: ([], []) for e in ENGS}
        self.dma_idx = {"sp": 0, "pool": 0}

    def _wait(self, eng, sem, val, wopno=None):
        if sem == eng and sem in ENGS and wopno is not None:
            if eng == "pe":
                return
            if self.drained[eng] >= wopno:
                return
            self.ops[eng].append(("d",))
            self.drained[eng] = self.opno[eng]
            return
        if val <= self.seen[eng][sem]:
            return
        self.seen[eng][sem] = val
        self.ops[eng].append(("w", sem, val))

    def _haz(self, eng, reads, writes):
        for k in reads:
            w = self.wr.get(k)
            if w is not None:
                self._wait(eng, w[0], w[1], w[2] if len(w) > 2 else None)
        for k in writes:
            w = self.wr.get(k)
            if w is not None:
                self._wait(eng, w[0], w[1], w[2] if len(w) > 2 else None)
            for sem, vv in self.rd.get(k, {}).items():
                if isinstance(vv, tuple):
                    self._wait(eng, sem, vv[0], vv[1])
                else:
                    self._wait(eng, sem, vv)

    def emit(self, eng, build, reads=(), writes=(), inc=1):
        if eng in ("sp", "pool"):
            self._haz(eng, reads, writes)
            i = self.dma_idx[eng]
            self.dma_idx[eng] = i + 1
            slot = f"{eng}{i % NSLOT}"
            self._wait(eng, slot, 16 * (i // NSLOT))
            self.cnt[slot] += 16
            val = self.cnt[slot]
            self.ops[eng].append(("o", build, ("dma", slot)))
            for k in reads:
                self.rd.setdefault(k, {})[slot] = val
            for k in writes:
                self.wr[k] = (slot, val)
                self.rd[k] = {}
            return
        self._haz(eng, reads, writes)
        self.opno[eng] += 1
        if inc == 0:
            pr, pw = self.pend[eng]
            pr.extend(reads)
            pw.extend(writes)
            self.ops[eng].append(("o", build, None))
            return
        self.cnt[eng] += inc
        n = self.cnt[eng]
        o = self.opno[eng]
        self.ops[eng].append(("o", build, ("eng", eng)))
        pr, pw = self.pend[eng]
        for k in list(reads) + pr:
            self.rd.setdefault(k, {})[eng] = (n, o)
        for k in list(writes) + pw:
            self.wr[k] = (eng, n, o)
            self.rd[k] = {}
        self.pend[eng] = ([], [])


def _build(nch=S // C):
    Sx = nch * C
    nc = bass.Bass("TRN2", target_bir_lowering=False)

    xd = nc.dram_tensor("x", [BPC, Sx, D], F32, kind="ExternalInput")
    wqT = nc.dram_tensor("wqT", [D, M], F32, kind="ExternalInput")
    wkT = nc.dram_tensor("wkT", [D, M], F32, kind="ExternalInput")
    wvT = nc.dram_tensor("wvT", [D, M], F32, kind="ExternalInput")
    woT = nc.dram_tensor("woT", [M, D], F32, kind="ExternalInput")
    bkc = nc.dram_tensor("bkc", [M, 1], F32, kind="ExternalInput")
    bqc = nc.dram_tensor("bqc", [M, 1], F32, kind="ExternalInput")
    bvc = nc.dram_tensor("bvc", [M, 1], F32, kind="ExternalInput")
    bor = nc.dram_tensor("bor", [1, D], F32, kind="ExternalInput")
    onesr = nc.dram_tensor("onesr", [1, 128], F32, kind="ExternalInput")
    wgb = nc.dram_tensor("wgb", [NJ, ND, 128, 128], I32, kind="ExternalInput")
    wub = nc.dram_tensor("wub", [NJ, ND, 128, 128], I32, kind="ExternalInput")
    wdb = nc.dram_tensor("wdb", [NJ, 128, D], I32, kind="ExternalInput")
    identd = nc.dram_tensor("identd", [128, 128], F32, kind="ExternalInput")
    lamqd = nc.dram_tensor("lamqd", [128, GPB * C], F32, kind="ExternalInput")
    wmtd = nc.dram_tensor("wmtd", [128, 128], F32, kind="ExternalInput")
    lamkd = nc.dram_tensor("lamkd", [128, 1], F32, kind="ExternalInput")
    outd = nc.dram_tensor("out", [BPC, Sx, D], F32, kind="ExternalOutput")

    es = ExitStack()
    sbuf = lambda nm, sh, dt: es.enter_context(nc.sbuf_tensor(nm, sh, dt))
    psum = lambda nm, sh, dt: es.enter_context(nc.psum_tensor(nm, sh, dt))

    with es:
        wg_sb = sbuf("wg_sb", [128, NJ * 1024], BF16)
        wu_sb = sbuf("wu_sb", [128, NJ * 1024], BF16)
        wd_sb = sbuf("wd_sb", [128, NJ * 1024], BF16)
        wq_sb = sbuf("wq_sb", [128, ND * 128], BF16)
        wk_sb = sbuf("wk_sb", [128, ND * 128], BF16)
        wv_sb = sbuf("wv_sb", [128, ND * 128], BF16)
        wo_sb = sbuf("wo_sb", [128, D], BF16)
        X = [sbuf("X0", [128, GPB * D], F32), sbuf("X1", [128, GPB * D], F32),
             sbuf("X2", [128, GPB * D], F32)]
        xhat = [sbuf("xhat0", [128, D], BF16), sbuf("xhat1", [128, D], BF16)]
        x1T = sbuf("x1T", [128, ND * GPB * C], BF16)
        x3T = sbuf("x3T", [128, ND * GPB * C], BF16)
        qt = sbuf("qt", [128, GPB * C], BF16)
        kt = sbuf("kt", [128, GPB * C], BF16)
        vt = sbuf("vt", [128, GPB * C], BF16)
        khat = sbuf("khat", [128, GPB * C], BF16)
        vtm = sbuf("vtm", [128, GPB * C], BF16)
        pwt = sbuf("pwt", [128, GPB * C], BF16)
        ot = sbuf("ot", [128, GPB * C], BF16)
        stf = sbuf("stf", [128, BPC * C], F32)
        stb = sbuf("stb", [128, BPC * C], BF16)
        ytA = sbuf("ytA", [128, D], BF16)
        ytB = sbuf("ytB", [128, D], BF16)
        hT = sbuf("hT", [128, 2 * GPB * C], BF16)
        rowsm = sbuf("rowsm", [128, D], BF16)
        ones_sb = sbuf("ones_sb", [1, 128], BF16)
        identb = sbuf("identb", [128, 128], BF16)
        lamq = sbuf("lamq", [128, GPB * C], BF16)
        wmt = sbuf("wmt", [128, 128], BF16)
        lamk = sbuf("lamk", [128, 1], F32)
        bk_s = sbuf("bk_s", [M, 1], F32)
        bq_s = sbuf("bq_s", [M, 1], F32)
        bv_s = sbuf("bv_s", [M, 1], F32)
        _bn6A = sbuf("bn6_A", [128, 12 * GPB], F32)
        _bn6B = sbuf("bn6_B", [128, 12 * GPB], F32)
        bn6 = [_bn6A, _bn6B, _bn6A, _bn6B]
        agg = [sbuf(f"agg_{i}", [128, 2 * GPB], F32) for i in range(4)]
        uu = [sbuf(f"uu_{i}", [128, GPB], F32) for i in range(4)]
        tt = [sbuf(f"tt_{i}", [128, GPB], F32) for i in range(4)]
        ui = [sbuf(f"ui_{i}", [128, GPB], I32) for i in range(4)]
        rr = [sbuf(f"rr_{i}", [128, GPB], F32) for i in range(4)]
        mrc = [sbuf(f"mrc_{i}", [128, GPB], F32) for i in range(4)]
        stx = [sbuf(f"stx_{i}", [128, 4 * GPB], F32) for i in (0, 2)]
        stxm = {0: 0, 2: 1}

        bank = [psum(f"pb{i}", [128, 512], F32) for i in range(8)]
        bankbf = [b[:].bitcast(BF16) for b in bank]

        P = Plan()
        sems = {}
        for s in P.sem_names:
            sems[s] = es.enter_context(nc.semaphore(f"sem_{s}"))

        def mm(build, reads, wbank, inc):
            P.emit("pe", build, reads=reads, writes=(("bank", wbank),), inc=inc)

        P.emit("pool", lambda E: E.dma_start(wq_sb[:], wqT[:, :].rearrange("(dc p) m -> p dc m", p=128)), writes=(("wq",),))
        P.emit("pool", lambda E: E.dma_start(wk_sb[:], wkT[:, :].rearrange("(dc p) m -> p dc m", p=128)), writes=(("wk",),))
        P.emit("pool", lambda E: E.dma_start(wv_sb[:], wvT[:, :].rearrange("(dc p) m -> p dc m", p=128)), writes=(("wv",),))
        P.emit("pool", lambda E: E.dma_start(wo_sb[:], woT[:, :]), writes=(("wo",),))
        P.emit("pool", lambda E: E.dma_start(identb[:], identd[:, :]), writes=(("ident",),))
        P.emit("pool", lambda E: E.dma_start(lamq[:], lamqd[:, :]), writes=(("lamq",),))
        P.emit("pool", lambda E: E.dma_start(wmt[:], wmtd[:, :]), writes=(("wmt",),))
        P.emit("pool", lambda E: E.dma_start(rowsm[0:1, :], bor[:, :]), writes=(("bo",),))
        P.emit("pool", lambda E: E.dma_start(ones_sb[:, :], onesr[:, :]), writes=(("ones",),))
        P.emit("sp", lambda E: E.dma_start(lamk[:], lamkd[:, :]), writes=(("lamk",),))
        P.emit("sp", lambda E: E.dma_start(bk_s[:], bkc[:, :]), writes=(("bk",),))
        P.emit("sp", lambda E: E.dma_start(bq_s[:], bqc[:, :]), writes=(("bq",),))
        P.emit("sp", lambda E: E.dma_start(bv_s[:], bvc[:, :]), writes=(("bv",),))
        P.emit("dve", lambda E: E.memset(stf[:], 0.0), writes=tuple(("stf", b) for b in range(BPC)))
        P.emit("dve", lambda E: E.memset(stb[:], 0.0), writes=tuple(("stb", b) for b in range(BPC)))

        for j in range(NJ):
            src_g = wgb[j].transpose([1, 0, 2])
            src_u = wub[j].transpose([1, 0, 2])
            P.emit("pool", lambda E, s=src_g, j=j: E.dma_start(
                wg_sb[:, j * 1024:(j + 1) * 1024].rearrange("p (dc h) -> p dc h", dc=ND), s), writes=(("wg", j),))
            P.emit("pool", lambda E, s=src_u, j=j: E.dma_start(
                wu_sb[:, j * 1024:(j + 1) * 1024].rearrange("p (dc h) -> p dc h", dc=ND), s), writes=(("wu", j),))
            P.emit("pool", lambda E, j=j: E.dma_start(wd_sb[:, j * 1024:(j + 1) * 1024], wdb[j]), writes=(("wd", j),))

        def rsqrt_step_list(pt, cols, eps):
            u, t_, i_, r_ = uu[pt], tt[pt], ui[pt], rr[pt]
            lo, hi = cols.start, cols.stop
            ku, kt_, ki = ("uu", pt, lo), ("tt", pt, lo), ("ui", pt, lo)
            vsrc = agg[pt][:, 2 * lo + 1: 2 * hi: 2]
            vkeys = tuple(("agg", pt, b) for b in range(lo, hi))
            cs = slice(lo, hi)
            y = i_[:, cs].bitcast(F32)
            steps = [
                lambda: P.emit("dve", lambda E: E.tensor_scalar(u[:, cs], vsrc, eps, None, ALU.add), reads=vkeys, writes=(ku,)),
                lambda: P.emit("dve", lambda E: E.tensor_copy(t_[:, cs], u[:, cs].bitcast(I32)), reads=(ku,), writes=(kt_,)),
                lambda: P.emit("dve", lambda E: E.tensor_scalar(t_[:, cs], t_[:, cs], -0.5, MAGIC, ALU.mult, ALU.add), reads=(kt_,), writes=(kt_,)),
                lambda: P.emit("dve", lambda E: E.tensor_copy(i_[:, cs], t_[:, cs]), reads=(kt_,), writes=(ki,)),
            ]
            for _ in range(3):
                steps += [
                    lambda: P.emit("dve", lambda E: E.tensor_mul(t_[:, cs], y, y), reads=(ki,), writes=(kt_,)),
                    lambda: P.emit("dve", lambda E: E.tensor_mul(t_[:, cs], t_[:, cs], u[:, cs]), reads=(kt_, ku), writes=(kt_,)),
                    lambda: P.emit("dve", lambda E: E.tensor_scalar(t_[:, cs], t_[:, cs], -0.5, 1.5, ALU.mult, ALU.add), reads=(kt_,), writes=(kt_,)),
                    lambda: P.emit("dve", lambda E: E.tensor_mul(y, y, t_[:, cs]), reads=(ki, kt_), writes=(ki,)),
                ]
            steps.append(lambda: P.emit("dve", lambda E: E.tensor_copy(r_[:, cs], y), reads=(ki,),
                                        writes=tuple(("rr", pt, b) for b in range(lo, hi))))
            return steps

        def bn_from(pt, lb, src_halves, src_keys):
            steps = []
            for hh in (0, 1):
                steps.append(lambda hh=hh: P.emit("dve", lambda E: E.bn_stats(
                    bn6[pt][:, lb * 12 + hh * 6: lb * 12 + hh * 6 + 6], src_halves[hh]),
                    reads=src_keys[hh], writes=(("bn6", pt, lb, hh),)))
            steps.append(lambda: P.emit("dve", lambda E: E.bn_aggr(
                agg[pt][:, 2 * lb:2 * lb + 2], bn6[pt][:, lb * 12: lb * 12 + 12]),
                reads=(("bn6", pt, lb, 0), ("bn6", pt, lb, 1)), writes=(("agg", pt, lb),)))
            return steps

        def stats_act(pt, lb, src_halves, src_keys, trash_ap, trash_key):
            sx = stx[stxm[pt]]
            steps = []
            for hh in (0, 1):
                steps.append(lambda hh=hh: P.emit("act", lambda E: E.activation(
                    trash_ap, src_halves[hh], AF.Identity,
                    accum_out=sx[:, lb * 4 + hh: lb * 4 + hh + 1]),
                    reads=src_keys[hh], writes=(trash_key, ("x1T", 0, 1), ("sx", pt, lb, hh))))
                steps.append(lambda hh=hh: P.emit("act", lambda E: E.activation(
                    trash_ap, src_halves[hh], AF.Square,
                    accum_out=sx[:, lb * 4 + 2 + hh: lb * 4 + 3 + hh]),
                    reads=src_keys[hh], writes=(trash_key, ("x1T", 0, 1), ("sx", pt, lb, 2 + hh))))
            tcol = tt[pt][:, lb:lb + 1]
            mcol = agg[pt][:, 2 * lb:2 * lb + 1]
            vcol = agg[pt][:, 2 * lb + 1:2 * lb + 2]
            mmcol = mrc[pt][:, lb:lb + 1]
            steps.append(lambda: P.emit("dve", lambda E: E.tensor_add(
                tcol, sx[:, lb * 4:lb * 4 + 1], sx[:, lb * 4 + 1:lb * 4 + 2]),
                reads=(("sx", pt, lb, 0), ("sx", pt, lb, 1)), writes=(("tt", pt, lb),)))
            steps.append(lambda: P.emit("dve", lambda E: E.tensor_scalar(
                mcol, tcol, 1.0 / D, None, ALU.mult),
                reads=(("tt", pt, lb),), writes=(("aggm", pt, lb),)))
            steps.append(lambda: P.emit("dve", lambda E: E.tensor_add(
                tcol, sx[:, lb * 4 + 2:lb * 4 + 3], sx[:, lb * 4 + 3:lb * 4 + 4]),
                reads=(("sx", pt, lb, 2), ("sx", pt, lb, 3)), writes=(("tt", pt, lb),)))
            steps.append(lambda: P.emit("dve", lambda E: E.tensor_mul(mmcol, mcol, mcol),
                reads=(("aggm", pt, lb),), writes=(("mrc", pt, lb),)))
            steps.append(lambda: P.emit("dve", lambda E: E.scalar_tensor_tensor(
                vcol, tcol, 1.0 / D, mmcol, ALU.mult, ALU.subtract),
                reads=(("tt", pt, lb), ("mrc", pt, lb)), writes=(("agg", pt, lb),)))
            return steps

        def mr_step(pt, lb):
            return lambda: P.emit("dve", lambda E: E.tensor_mul(
                mrc[pt][:, lb:lb + 1], agg[pt][:, 2 * lb:2 * lb + 1], rr[pt][:, lb:lb + 1]),
                reads=(("agg", pt, lb), ("aggm", pt, lb), ("rr", pt, lb)), writes=(("mrc", pt, lb),))

        def build_A(ti, g, cidx):
            par = ti % 3
            Xc = X[par]
            A = []

            def xk(lb):
                return ("X", par, lb)

            Adma = []
            A.append(("EPI_BARRIER", ti - 3))
            for lb in range(GPB):
                gb = 2 * g + lb
                A.append(lambda lb=lb, gb=gb: P.emit(
                    "sp", lambda E: E.dma_start(Xc[:, lb * D:(lb + 1) * D], xd[gb, cidx * C:(cidx + 1) * C, :]),
                    writes=(xk(lb),)))
            for lb in range(GPB):
                A += stats_act(0, lb, [Xc[:, lb * D:lb * D + 512], Xc[:, lb * D + 512:lb * D + 1024]],
                               [(xk(lb),), (xk(lb),)], x1T[:, 0:512], ("x1T", 0, 0))
            A += rsqrt_step_list(0, slice(0, GPB), 1e-5)
            A.append(("PRO_BARRIER", ti - 1))
            for lb in range(GPB):
                A.append(lambda lb=lb: P.emit("dve", lambda E: E.tensor_scalar(
                    xhat[lb][:], Xc[:, lb * D:(lb + 1) * D],
                    agg[0][:, 2 * lb:2 * lb + 1], rr[0][:, lb:lb + 1], ALU.subtract, ALU.mult),
                    reads=(xk(lb), ("agg", 0, lb), ("aggm", 0, lb), ("rr", 0, lb)), writes=(("xhat", lb),)))
            for lb in range(GPB):
                bk_ = 4 + lb

                def trs(lb=lb, bk_=bk_):
                    for dc in range(ND):
                        P.emit("pe", lambda E, dc=dc, lb=lb, bk_=bk_: E.transpose(
                            bankbf[bk_][:, dc * 128:(dc + 1) * 128], xhat[lb][:, dc * 128:(dc + 1) * 128], identb[:]),
                            reads=(("xhat", lb), ("ident",)), writes=(("bank", bk_),), inc=1 if dc == ND - 1 else 0)
                A.append(trs)
                A.append(lambda lb=lb, bk_=bk_: P.emit("dve", lambda E: E.tensor_copy(
                    x1T[:].rearrange("p (dc bt) -> p dc bt", dc=ND)[:, :, lb * 128:(lb + 1) * 128],
                    bankbf[bk_].rearrange("p (dc t) -> p dc t", dc=ND)),
                    reads=(("bank", bk_),), writes=tuple(("x1T", dc, lb) for dc in range(ND))))
            x1keys = tuple(("x1T", dc, lb) for dc in range(ND) for lb in range(GPB))

            def proj(wsb, wkey, bnk):
                def f():
                    for dc in range(ND):
                        mm(lambda E, dc=dc: E.matmul(
                            bank[bnk][:, 0:GPB * C], wsb[:, dc * 128:(dc + 1) * 128],
                            x1T[:, dc * GPB * C:(dc + 1) * GPB * C],
                            start=dc == 0, stop=dc == ND - 1, skip_group_check=True),
                            reads=(wkey,) + x1keys, wbank=bnk, inc=1 if dc == ND - 1 else 0)
                return f

            A.append(proj(wq_sb, ("wq",), 4))
            A.append(lambda: P.emit("act", lambda E: E.activation(
                qt[:], bank[4][:, 0:GPB * C], AF.Identity, bias=bq_s[:]),
                reads=(("bank", 4), ("bq",)), writes=(("qt",),)))
            A.append(lambda: P.emit("dve", lambda E: E.tensor_tensor(
                qt[:], qt[:], lamq[:], ALU.mult),
                reads=(("qt",), ("lamq",)), writes=(("qt",),)))
            A.append(proj(wk_sb, ("wk",), 5))
            A.append(lambda: P.emit("act", lambda E: E.activation(
                kt[:], bank[5][:, 0:GPB * C], AF.Identity, bias=bk_s[:]),
                reads=(("bank", 5), ("bk",)), writes=(("kt",),)))
            A.append(proj(wv_sb, ("wv",), 4))
            A.append(lambda: P.emit("act", lambda E: E.activation(
                vt[:], bank[4][:, 0:GPB * C], AF.Identity, bias=bv_s[:]),
                reads=(("bank", 4), ("bv",)), writes=(("vt",),)))
            for lb in range(GPB):
                A.append(lambda lb=lb: P.emit("pe", lambda E: E.transpose(
                    bankbf[5][:, lb * 128:(lb + 1) * 128], kt[:, lb * 128:(lb + 1) * 128], identb[:]),
                    reads=(("kt",), ("ident",)), writes=(("bank", 5),)))
                A.append(lambda lb=lb: P.emit("pe", lambda E: E.transpose(
                    bankbf[4][:, lb * 128:(lb + 1) * 128], vt[:, lb * 128:(lb + 1) * 128], identb[:]),
                    reads=(("vt",), ("ident",)), writes=(("bank", 4),)))
            for lb in range(GPB):
                A.append(lambda lb=lb: P.emit("dve", lambda E: E.tensor_scalar(
                    khat[:, lb * 128:(lb + 1) * 128], bankbf[5][:, lb * 128:(lb + 1) * 128], lamk[:], None, ALU.mult),
                    reads=(("bank", 5), ("lamk",)), writes=(("khat", lb),)))
                A.append(lambda lb=lb: P.emit("act", lambda E: E.activation(
                    vtm[:, lb * 128:(lb + 1) * 128], bankbf[4][:, lb * 128:(lb + 1) * 128], AF.Identity),
                    reads=(("bank", 4),), writes=(("vtm", lb),)))
            for lb in range(GPB):
                gb = 2 * g + lb
                bs = slice(lb * 128, (lb + 1) * 128)
                A.append(lambda bs=bs: P.emit("pe", lambda E: E.matmul(
                    bank[5][:, 0:128], kt[:, bs], qt[:, bs], start=True, stop=True, skip_group_check=True),
                    reads=(("kt",), ("qt",)), writes=(("bank", 5),)))
                A.append(lambda lb=lb, bs=bs: P.emit("dve", lambda E: E.tensor_tensor(
                    pwt[:, bs], bank[5][:, 0:128], wmt[:], ALU.mult),
                    reads=(("bank", 5), ("wmt",)), writes=(("pwt", lb),)))

                def omm(lb=lb, gb=gb, bs=bs):
                    gs = slice(gb * 128, (gb + 1) * 128)
                    mm(lambda E: E.matmul(bank[4][:, 0:128], vtm[:, bs], pwt[:, bs],
                                          start=True, stop=False, skip_group_check=True),
                       reads=(("vtm", lb), ("pwt", lb)), wbank=4, inc=0)
                    mm(lambda E: E.matmul(bank[4][:, 0:128], stb[:, gs], qt[:, bs],
                                          start=False, stop=True, skip_group_check=True),
                       reads=(("stb", gb), ("qt",)), wbank=4, inc=1)
                A.append(omm)
                A.append(lambda lb=lb, bs=bs: P.emit("act", lambda E: E.activation(
                    ot[:, bs], bank[4][:, 0:128], AF.Identity), reads=(("bank", 4),), writes=(("ot", lb),)))
                A.append(lambda lb=lb, bs=bs: P.emit("pe", lambda E: E.matmul(
                    bank[5][:, 128:256], khat[:, bs], vtm[:, bs], start=True, stop=True, skip_group_check=True),
                    reads=(("khat", lb), ("vtm", lb)), writes=(("bank", 5),)))

                def stup(gb=gb):
                    gs = slice(gb * 128, (gb + 1) * 128)
                    P.emit("dve", lambda E: E.scalar_tensor_tensor(
                        stf[:, gs], stf[:, gs], LAMC, bank[5][:, 128:256], ALU.mult, ALU.add),
                        reads=(("stf", gb), ("bank", 5)), writes=(("stf", gb),))
                    P.emit("dve", lambda E: E.tensor_copy(stb[:, gs], stf[:, gs]),
                           reads=(("stf", gb),), writes=(("stb", gb),))
                A.append(stup)
            for lb in range(GPB):
                bs = slice(lb * 128, (lb + 1) * 128)

                def y1mm(lb=lb, bs=bs):
                    for dh in (0, 1):
                        mm(lambda E, dh=dh: E.matmul(bank[4 + dh][:], ot[:, bs],
                                                     wo_sb[:, dh * 512:(dh + 1) * 512],
                                                     start=True, stop=False, skip_group_check=True),
                           reads=(("ot", lb), ("wo",)), wbank=4 + dh, inc=0)
                        mm(lambda E, dh=dh: E.matmul(bank[4 + dh][:], ones_sb[0:1, :],
                                                     rowsm[0:1, dh * 512:(dh + 1) * 512],
                                                     start=False, stop=True, skip_group_check=True),
                           reads=(("bo",), ("ones",)), wbank=4 + dh, inc=1)
                A.append(y1mm)
                A += bn_from(1, lb, [bank[4][:], bank[5][:]], [(("bank", 4),), (("bank", 5),)])
                A += rsqrt_step_list(1, slice(lb, lb + 1), 1e-5)
                A.append(mr_step(1, lb))

                def apply2(lb=lb):
                    P.emit("dve", lambda E: E.tensor_scalar(
                        Xc[:, lb * D:(lb + 1) * D], Xc[:, lb * D:(lb + 1) * D],
                        mrc[1][:, lb:lb + 1], None, ALU.subtract),
                        reads=(xk(lb), ("mrc", 1, lb)), writes=(xk(lb),))
                    for dh in (0, 1):
                        P.emit("dve", lambda E, dh=dh: E.scalar_tensor_tensor(
                            Xc[:, lb * D + dh * 512: lb * D + (dh + 1) * 512],
                            bank[4 + dh][:], rr[1][:, lb:lb + 1],
                            Xc[:, lb * D + dh * 512: lb * D + (dh + 1) * 512],
                            ALU.mult, ALU.add),
                            reads=(("bank", 4 + dh), ("rr", 1, lb), (xk(lb),)), writes=(xk(lb),))
                A.append(apply2)
                A += bn_from(2, lb, [Xc[:, lb * D:lb * D + 512], Xc[:, lb * D + 512:lb * D + 1024]],
                             [(xk(lb),), (xk(lb),)])
            A += rsqrt_step_list(2, slice(0, GPB), 1e-5)
            for lb in range(GPB):
                A.append(lambda lb=lb: P.emit("dve", lambda E: E.tensor_scalar(
                    xhat[lb][:], Xc[:, lb * D:(lb + 1) * D],
                    agg[2][:, 2 * lb:2 * lb + 1], rr[2][:, lb:lb + 1], ALU.subtract, ALU.mult),
                    reads=(xk(lb), ("agg", 2, lb), ("aggm", 2, lb), ("rr", 2, lb)),
                    writes=(("xhat", lb),)))
            A.append(("TILE_DONE", ti))
            return A

        EPI_BANKS = deque()
        EPI_REST = deque()
        AQ = deque()

        a_done = set()
        epi_done = set()
        pro_done = set()

        def drain(q, k):
            while q and k > 0:
                h = q[0]
                if isinstance(h, tuple) and h[0] == "EPI_BARRIER":
                    if h[1] >= 0 and h[1] not in epi_done:
                        return
                    q.popleft()
                    continue
                if isinstance(h, tuple) and h[0] == "TILE_DONE":
                    a_done.add(h[1])
                    q.popleft()
                    continue
                if isinstance(h, tuple) and h[0] == "EPI_DONE":
                    epi_done.add(h[1])
                    q.popleft()
                    continue
                if isinstance(h, tuple) and h[0] == "PRO_BARRIER":
                    if h[1] >= 0 and h[1] not in pro_done:
                        return
                    q.popleft()
                    continue
                q.popleft()()
                k -= 1

        def drain_until_tile(ti):
            while ti not in a_done:
                h = AQ[0]
                if isinstance(h, tuple) and h[0] == "PRO_BARRIER":
                    assert h[1] < 0 or h[1] in pro_done, (h,)
                    AQ.popleft()
                    continue
                if isinstance(h, tuple) and h[0] == "EPI_BARRIER":
                    if h[1] >= 0 and h[1] not in epi_done:
                        drain(EPI_BANKS, 10 ** 9)
                        drain(EPI_REST, 10 ** 9)
                        assert h[1] < 0 or h[1] in epi_done, (h,)
                    AQ.popleft()
                    continue
                if isinstance(h, tuple) and h[0] == "TILE_DONE":
                    a_done.add(h[1])
                    AQ.popleft()
                    continue
                AQ.popleft()()

        def emit_F(ti, g, cidx):
            par = ti % 3
            Xc = X[par]

            def xk(lb):
                return ("X", par, lb)

            drain_until_tile(ti)
            drain(EPI_BANKS, 99)
            for lb in range(GPB):
                bk_ = 2 + lb
                for dc in range(ND):
                    P.emit("pe", lambda E, lb=lb, dc=dc, bk_=bk_: E.transpose(
                        bankbf[bk_][:, dc * 128:(dc + 1) * 128], xhat[lb][:, dc * 128:(dc + 1) * 128], identb[:]),
                        reads=(("xhat", lb), ("ident",)), writes=(("bank", bk_),), inc=1 if dc == ND - 1 else 0)
                P.emit("dve", lambda E, lb=lb, bk_=bk_: E.tensor_copy(
                    x3T[:].rearrange("p (dc bt) -> p dc bt", dc=ND)[:, :, lb * 128:(lb + 1) * 128],
                    bankbf[bk_].rearrange("p (dc t) -> p dc t", dc=ND)),
                    reads=(("bank", bk_),), writes=tuple(("x3T", dc, lb) for dc in range(ND)))
            x3keys = tuple(("x3T", dc, lb) for dc in range(ND) for lb in range(GPB))
            pro_done.add(ti)

            def emit_down(j):
                jp = j % 2
                for lb in range(GPB):
                    for dh in range(2):
                        dbk = (6, 7, 0, 1)[lb * 2 + dh]
                        mm(lambda E, j=j, jp=jp, lb=lb, dh=dh, dbk=dbk: E.matmul(
                            bank[dbk][:], hT[:, jp * 256 + lb * 128: jp * 256 + (lb + 1) * 128],
                            wd_sb[:, j * 1024 + dh * 512: j * 1024 + (dh + 1) * 512],
                            start=j == 0, stop=j == NJ - 1, skip_group_check=True),
                            reads=(("hT", jp), ("wd", j)), wbank=dbk,
                            inc=1 if j == NJ - 1 else 0)

            for j in range(NJ):
                jp = j % 2
                for dc in range(ND):
                    mm(lambda E, j=j, dc=dc: E.matmul(
                        bank[2][:, 0:256], wg_sb[:, j * 1024 + dc * 128: j * 1024 + (dc + 1) * 128],
                        x3T[:, dc * 256:(dc + 1) * 256], start=dc == 0, stop=dc == ND - 1,
                        skip_group_check=True),
                        reads=(("wg", j),) + x3keys, wbank=2, inc=1 if dc == ND - 1 else 0)
                P.emit("act", lambda E, jp=jp: E.activation(
                    hT[:, jp * 256:(jp + 1) * 256], bank[2][:, 0:256], AF.Silu, scale=1.0 / 127.0),
                    reads=(("bank", 2),), writes=(("hT", jp),))
                if j >= 1:
                    drain(AQ, 3)
                for dc in range(ND):
                    mm(lambda E, j=j, dc=dc: E.matmul(
                        bank[3][:, 0:256], wu_sb[:, j * 1024 + dc * 128: j * 1024 + (dc + 1) * 128],
                        x3T[:, dc * 256:(dc + 1) * 256], start=dc == 0, stop=dc == ND - 1,
                        skip_group_check=True),
                        reads=(("wu", j),) + x3keys, wbank=3, inc=1 if dc == ND - 1 else 0)
                if j > 0:
                    emit_down(j - 1)
                if j >= 1:
                    drain(AQ, 2)
                P.emit("dve", lambda E, jp=jp: E.tensor_tensor(
                    hT[:, jp * 256:(jp + 1) * 256], hT[:, jp * 256:(jp + 1) * 256], bank[3][:, 0:256], ALU.mult),
                    reads=(("hT", jp), ("bank", 3)), writes=(("hT", jp),))
                drain(EPI_REST, 4)
                if j >= 1:
                    drain(AQ, 3)
            emit_down(NJ - 1)

            yts = (ytA, ytB)
            for lb in range(GPB):
                for dh in range(2):
                    dbk = (6, 7, 0, 1)[lb * 2 + dh]
                    EPI_BANKS.append(lambda lb=lb, dh=dh, dbk=dbk: P.emit(
                        "act", lambda E: E.activation(yts[lb][:, dh * 512:(dh + 1) * 512], bank[dbk][:], AF.Identity),
                        reads=(("bank", dbk),), writes=(("yt", lb, dh),)))
            for lb in range(GPB):
                for st in bn_from(3, lb, [yts[lb][:, 0:512], yts[lb][:, 512:1024]],
                                  [(("yt", lb, 0),), (("yt", lb, 1),)]):
                    EPI_REST.append(st)
            for st in rsqrt_step_list(3, slice(0, GPB), EPS4):
                EPI_REST.append(st)
            for lb in range(GPB):
                EPI_REST.append(mr_step(3, lb))
            for lb in range(GPB):
                gb = 2 * g + lb

                def apply4(lb=lb):
                    for dh in (0, 1):
                        P.emit("dve", lambda E, dh=dh: E.tensor_scalar(
                            yts[lb][:, dh * 512:(dh + 1) * 512], yts[lb][:, dh * 512:(dh + 1) * 512],
                            rr[3][:, lb:lb + 1], mrc[3][:, lb:lb + 1], ALU.mult, ALU.subtract),
                            reads=(("yt", lb, dh), ("rr", 3, lb), ("mrc", 3, lb)), writes=(("yt", lb, dh),))
                        P.emit("dve", lambda E, dh=dh: E.tensor_tensor(
                            Xc[:, lb * D + dh * 512: lb * D + (dh + 1) * 512],
                            Xc[:, lb * D + dh * 512: lb * D + (dh + 1) * 512],
                            yts[lb][:, dh * 512:(dh + 1) * 512], ALU.add),
                            reads=(("yt", lb, dh), (xk(lb),)), writes=(xk(lb),))
                EPI_REST.append(apply4)
                EPI_REST.append(lambda lb=lb, gb=gb: P.emit(
                    "sp", lambda E: E.dma_start(outd[gb, cidx * C:(cidx + 1) * C, :], Xc[:, lb * D:(lb + 1) * D]),
                    reads=(xk(lb),)))
            EPI_REST.append(("EPI_DONE", ti))

        tiles = [(g, c) for c in range(nch) for g in range(NGRP)]
        AQ.extend(build_A(0, *tiles[0]))
        for ti, (g, c) in enumerate(tiles):
            if ti + 1 < len(tiles):
                AQ.extend(build_A(ti + 1, *tiles[ti + 1]))
            emit_F(ti, g, c)
        if tiles:
            drain_until_tile(len(tiles) - 1)
        while EPI_BANKS or EPI_REST or AQ:
            drain(EPI_BANKS, 10 ** 9)
            drain(EPI_REST, 10 ** 9)
            drain(AQ, 10 ** 9)

        for i in range(NSLOT):
            for q in ("sp", "pool"):
                s_ = f"{q}{i}"
                if P.cnt[s_]:
                    P.ops["sp"].append(("w", s_, P.cnt[s_]))

        waited = {s: set() for s in P.sem_names}
        for e in ENGS:
            for item in P.ops[e]:
                if item[0] == "w":
                    waited[item[1]].add(item[2])
        for e in ENGS:
            kept = []
            n = 0
            newops = []
            for item in P.ops[e]:
                if item[0] == "o" and item[2] is not None and item[2][0] == "eng":
                    n += 1
                    if n in waited[item[2][1]]:
                        kept.append(n)
                        newops.append(item)
                    else:
                        newops.append(("o", item[1], None))
                else:
                    newops.append(item)
            P.ops[e] = newops
            P.kept_counts = getattr(P, "kept_counts", {})
            P.kept_counts[e] = kept
        import bisect
        for e in ENGS:
            out = []
            for item in P.ops[e]:
                if item[0] == "w" and item[1] in ENGS:
                    kc = P.kept_counts[item[1]]
                    nv = bisect.bisect_right(kc, item[2])
                    out.append(("w", item[1], nv))
                else:
                    out.append(item)
            P.ops[e] = out

        def replay(E, ename):
            for item in P.ops[ename]:
                if item[0] == "d":
                    E.drain()
                    continue
                if item[0] == "w":
                    E.wait_ge(sems[item[1]], item[2])
                else:
                    _, build, tag = item
                    ins = build(E)
                    if tag is not None:
                        kind, s_ = tag
                        ins.then_inc(sems[s_], 16 if kind == "dma" else 1)

        with nc.Block() as blk:
            blk.tensor(lambda E: replay(E, "pe"))
            blk.vector(lambda E: replay(E, "dve"))
            blk.scalar(lambda E: replay(E, "act"))
            blk.sync(lambda E: replay(E, "sp"))
            blk.gpsimd(lambda E: replay(E, "pool"))

    return nc


def _host_inputs(inputs, Sx=S):
    x = np.ascontiguousarray(np.asarray(inputs["L0_attn_x"], dtype=np.float32))
    wq = np.asarray(inputs["L0_attn_Wq"], np.float32)
    wk = np.asarray(inputs["L0_attn_Wk"], np.float32)
    wv = np.asarray(inputs["L0_attn_Wv"], np.float32)
    wo = np.asarray(inputs["L0_attn_Wo"], np.float32)
    bk = np.asarray(inputs["L0_attn_bk"], np.float32)
    bq = np.asarray(inputs["L0_attn_bq"], np.float32)
    bv = np.asarray(inputs["L0_attn_bv"], np.float32)
    bo = np.asarray(inputs["L0_attn_bo"], np.float32)
    wg = np.asarray(inputs["L0_ffn_w_gate"], np.int32)
    wu = np.asarray(inputs["L0_ffn_w_up"], np.int32)
    wd = np.asarray(inputs["L0_ffn_w_down"], np.int32)

    wqT = np.ascontiguousarray(wq.T)
    wkT = np.ascontiguousarray(wk.T)
    wvT = np.ascontiguousarray(wv.T)
    woT = np.ascontiguousarray(wo.T)
    wgb = np.ascontiguousarray(wg.T.reshape(ND, 128, NJ, 128).transpose(2, 0, 1, 3))
    wub = np.ascontiguousarray(wu.T.reshape(ND, 128, NJ, 128).transpose(2, 0, 1, 3))
    wdb = np.ascontiguousarray(wd.T.reshape(NJ, 128, D))
    bkc = np.ascontiguousarray(bk.reshape(M, 1))
    bqc = np.ascontiguousarray(bq.reshape(M, 1))
    bvc = np.ascontiguousarray(bv.reshape(M, 1))
    bor = np.ascontiguousarray(bo.reshape(1, D))
    onesr = np.ones((1, 128), np.float32)

    lam_i = (DECAY ** np.arange(C)).astype(np.float32)
    lamqh = np.ascontiguousarray(np.broadcast_to(np.tile(lam_i, GPB)[None, :], (128, GPB * C)).astype(np.float32))
    wmth = np.zeros((C, C), np.float32)
    srow = (LR * DECAY ** (-1.0 - np.arange(C))).astype(np.float32)
    for s_ in range(C):
        wmth[s_, s_ + 1:] = srow[s_]
    lamkh = np.ascontiguousarray((LR * DECAY ** (C - 1 - np.arange(C))).reshape(C, 1).astype(np.float32))
    identh = np.eye(128, dtype=np.float32)

    shared = dict(wqT=wqT, wkT=wkT, wvT=wvT, woT=woT, bkc=bkc, bqc=bqc, bvc=bvc,
                  bor=bor, onesr=onesr, wgb=wgb, wub=wub, wdb=wdb, identd=identh,
                  lamqd=lamqh, wmtd=wmth, lamkd=lamkh)
    nb = x.shape[0]
    ncores = nb // BPC
    in_maps = []
    for i in range(ncores):
        m = dict(shared)
        m["x"] = np.ascontiguousarray(x[i * BPC:(i + 1) * BPC, :Sx])
        in_maps.append(m)
    return in_maps


def kernel(**inputs):
    in_maps = _host_inputs(inputs)
    nc = _build(S // C)
    res = run_bass_kernel_spmd(nc, in_maps, core_ids=list(range(NCORES)))
    out = np.concatenate([r["out"] for r in res.results], axis=0)
    return out.astype(np.float32)
